# version 14
# baseline (speedup 1.0000x reference)
import numpy as np
import ml_dtypes
import concourse.bass as bass
import concourse.mybir as mybir
from concourse.bass_utils import run_bass_kernel_spmd

F32 = mybir.dt.float32
BF16 = mybir.dt.bfloat16
AF = mybir.ActivationFunctionType
ALU = mybir.AluOpType
AX = mybir.AxisListType

D = 2048
S = 2048
H = 16
HD = 128
INTER = 5632
T = 512
NT = S // T
BL = 2
KD = D // 128
KI = INTER // 128
KH = KI // 4
OGD = 8
OGI = 22
OGDN = 16
NCORES = 8
EPS = 1e-5
SLOTW = 4096


class Sem:
    def __init__(self, nc, name):
        self.h = nc.alloc_semaphore(name)
        self.count = 0
        self.name = name


class Region:
    def __init__(self):
        self.req = []

    def require(self):
        return list(self.req)

    def set(self, reqs):
        self.req = [r for r in reqs if r is not None]


class Plan:
    ENGINES = ("tensor", "vector", "scalar", "sync", "gpsimd")

    def __init__(self, nc):
        self.nc = nc
        self.streams = {e: [] for e in self.ENGINES}
        self.sems = []

    def sem(self, name):
        s = Sem(self.nc, name)
        self.sems.append(s)
        return s

    def op(self, engine, build, waits=(), inc=None, inc_n=1):
        wmap = {}
        for w in waits:
            if w is None:
                continue
            s, v = w
            assert v <= s.count, f"wait on {s.name} for {v} > emitted {s.count}"
            if v > 0 and (s.h not in wmap or wmap[s.h] < v):
                wmap[s.h] = v
        if inc is not None:
            inc.count += inc_n
            inc_h = inc.h
        else:
            inc_h = None
        wlist = list(wmap.items())

        def fn(eng, _wl=wlist, _build=build, _ih=inc_h, _n=inc_n):
            for h, v in _wl:
                eng.wait_ge(h, v)
            inst = _build(eng)
            if _ih is not None:
                inst.then_inc(_ih, _n)
        self.streams[engine].append(fn)

    def barrier(self, engines=("tensor", "vector", "scalar", "gpsimd"),
                only=None):
        snap = [(s.h, s.count) for s in self.sems
                if s.count > 0 and (only is None or s.name in only)]
        for e in engines:
            def fn(eng, _snap=list(snap)):
                for h, v in _snap:
                    eng.wait_ge(h, v)
            self.streams[e].append(fn)

    def finalize(self):
        nc = self.nc
        with nc.Block() as block:
            block.tensor(lambda t: [f(t) for f in self.streams["tensor"]])
            block.vector(lambda v: [f(v) for f in self.streams["vector"]])
            block.scalar(lambda s: [f(s) for f in self.streams["scalar"]])
            block.sync(lambda s: [f(s) for f in self.streams["sync"]])
            block.gpsimd(lambda g: [f(g) for f in self.streams["gpsimd"]])


class _StopEmit(Exception):
    pass


def build_nc(_dbg=None):
    nc = bass.Bass("TRN2", target_bir_lowering=False)

    x_fm = nc.dram_tensor("x_fm", [BL, D, S], F32, kind="ExternalInput")
    wq_f = nc.dram_tensor("wq_f", [OGD, 128, KD * 256], F32, kind="ExternalInput")
    wkv_f = nc.dram_tensor("wkv_f", [OGD, 128, KD * 256], F32, kind="ExternalInput")
    wo_f = nc.dram_tensor("wo_f", [OGD, 128, KD * 256], F32, kind="ExternalInput")
    wg_f = nc.dram_tensor("wg_f", [OGI, 128, KD * 256], F32, kind="ExternalInput")
    wu_f = nc.dram_tensor("wu_f", [OGI, 128, KD * 256], F32, kind="ExternalInput")
    wd_f = nc.dram_tensor("wd_f", [OGDN, 4, 128, KH * 128], F32, kind="ExternalInput")
    gs_in = nc.dram_tensor("gs_in", [128, KI], F32, kind="ExternalInput")
    us_in = nc.dram_tensor("us_in", [128, KI], F32, kind="ExternalInput")
    ds_in = nc.dram_tensor("ds_in", [128, KD], F32, kind="ExternalInput")
    cdft_in = nc.dram_tensor("cdft_in", [128, 128], BF16, kind="ExternalInput")
    sdft_in = nc.dram_tensor("sdft_in", [128, 128], BF16, kind="ExternalInput")
    ncdft_in = nc.dram_tensor("ncdft_in", [128, 128], BF16, kind="ExternalInput")
    maskA_in = nc.dram_tensor("maskA_in", [128, 256], BF16, kind="ExternalInput")
    maskB_in = nc.dram_tensor("maskB_in", [16, 2048], BF16, kind="ExternalInput")
    out_fm = nc.dram_tensor("out_fm", [BL, D, S], F32, kind="ExternalOutput")
    dbg_out = None
    if _dbg:
        dbg_t = {
            "dbg_qbf": ([128, 8192], BF16), "dbg_vh": ([128, 8192], BF16),
            "dbg_xbf": ([128, 8192], BF16), "dbg_sqf": ([16, 1024], F32),
            "dbg_a": ([128, 8192], F32), "dbg_b": ([128, 8192], F32),
            "dbg_hbf": ([128, 22528], BF16), "dbg_pref": ([128, 16], F32),
        }
        dbg_out = {k: nc.dram_tensor(k, sh, dt, kind="ExternalOutput")
                   for k, (sh, dt) in dbg_t.items()}

    wq_b = nc.dram_tensor("wq_b", [OGD, 128, KD * 256], BF16, kind="Internal")
    wkv_b = nc.dram_tensor("wkv_b", [OGD, 128, KD * 256], BF16, kind="Internal")
    wo_b = nc.dram_tensor("wo_b", [OGD, 128, KD * 256], BF16, kind="Internal")
    wg_b = nc.dram_tensor("wg_b", [OGI, 128, KD * 256], BF16, kind="Internal")
    wu_b = nc.dram_tensor("wu_b", [OGI, 128, KD * 256], BF16, kind="Internal")
    wd_b = nc.dram_tensor("wd_b", [OGDN, 4, 128, KH * 128], BF16, kind="Internal")

    A = nc.alloc_sbuf_tensor("buf_a", [128, KD * T], F32)
    B = nc.alloc_sbuf_tensor("buf_b", [128, KD * T], F32)
    Ctmp = nc.alloc_sbuf_tensor("buf_c", [128, 2 * T], F32)
    SQF = nc.alloc_sbuf_tensor("sq_f", [16, 2 * T], F32)
    SQB = nc.alloc_sbuf_tensor("sq_b", [16, 2 * T], BF16)
    ROWSF = SQF
    ROWSB = SQB
    PREF = nc.alloc_sbuf_tensor("prefix", [128, 16], F32)
    PREFT = nc.alloc_sbuf_tensor("preft", [128, 4], F32)
    XBF = nc.alloc_sbuf_tensor("x_bf", [128, KD * T], BF16)
    QBF = nc.alloc_sbuf_tensor("q_bf", [128, KD * T], BF16)
    VHBF = nc.alloc_sbuf_tensor("vh_bf", [128, KD * T], BF16)
    HBF = nc.alloc_sbuf_tensor("h_bf", [128, KI * T], BF16)
    WS = nc.alloc_sbuf_tensor("wslot", [128, 2 * SLOTW], BF16)
    cdft = nc.alloc_sbuf_tensor("cdft", [128, 128], BF16)
    sdft = nc.alloc_sbuf_tensor("sdft", [128, 128], BF16)
    ncdft = nc.alloc_sbuf_tensor("ncdft", [128, 128], BF16)
    maskA = nc.alloc_sbuf_tensor("maskA", [128, 256], BF16)
    maskB = nc.alloc_sbuf_tensor("maskB", [16, 2048], BF16)
    gs = nc.alloc_sbuf_tensor("gs", [128, KI], F32)
    us = nc.alloc_sbuf_tensor("us", [128, KI], F32)
    ds = nc.alloc_sbuf_tensor("ds", [128, KD], F32)
    ones1 = nc.alloc_sbuf_tensor("ones1", [1, 128], BF16)
    onescol = nc.alloc_sbuf_tensor("onescol", [128, 1], BF16)
    onest = nc.alloc_sbuf_tensor("onest", [128, T], BF16)

    SH = KD * T
    M_QC = SH + 0
    M_QS = SH + T
    M_KC = SH + 2 * T
    M_KS = SH + 3 * T
    M_P = SH + 4 * T
    M_KVN = SH + 12 * T
    M_KTB = SH + 14 * T
    M_SQ = SH + 16 * T
    M_DFT = SH + 18 * T

    PSA = nc.alloc_psum_tensor("psa", [128, 1024], F32)
    PSB = nc.alloc_psum_tensor("psb", [128, 1024], F32)
    PSC = nc.alloc_psum_tensor("psc", [128, 1024], F32)
    PSD = nc.alloc_psum_tensor("psd", [128, 1024], F32)

    p = Plan(nc)
    ST = p.sem("s_t")
    SV = p.sem("s_v")
    SA = p.sem("s_a")
    SW = p.sem("s_w")
    SC = p.sem("s_c")
    SG = p.sem("s_g")
    SS = p.sem("s_s")
    SP = p.sem("s_p")

    rSLOT = [Region(), Region()]
    rXBF = Region()
    rB_F32 = Region()

    for dst, src in ((cdft, cdft_in), (sdft, sdft_in), (ncdft, ncdft_in),
                     (maskA, maskA_in), (maskB, maskB_in),
                     (gs, gs_in), (us, us_in), (ds, ds_in)):
        p.op("sync", lambda e, d=dst, s=src: e.dma_start(out=d[:], in_=s.ap()),
             inc=SW, inc_n=16)
    consts_w = (SW, SW.count)
    p.op("gpsimd", lambda e: e.memset(ones1[:], 1.0), inc=SC)
    p.op("gpsimd", lambda e: e.memset(onescol[:], 1.0), inc=SC)
    p.op("gpsimd", lambda e: e.memset(onest[:], 1.0), inc=SC)
    memset_w = (SC, SC.count)

    def emit_xbf_load(b, s0, waits):
        src = x_fm.ap()[b, :, s0:s0 + T].rearrange("(c p) t -> p c t", p=128)
        dst = XBF[:].rearrange("p (c t) -> p c t", c=KD)
        p.op("gpsimd", lambda e, s=src, d=dst: e.dma_start(out=d, in_=s),
             waits=waits, inc=SG, inc_n=16)

    def emit_xf_load(b, s0, waits):
        src = x_fm.ap()[b, :, s0:s0 + T].rearrange("(c p) t -> p c t", p=128)
        dst = B[:].rearrange("p (c t) -> p c t", c=KD)
        p.op("scalar", lambda e, s=src, d=dst: e.dma_start(out=d, in_=s),
             waits=waits, inc=SS, inc_n=16)

    emit_xbf_load(0, 0, [])
    emit_xf_load(0, 0, [])

    cast_order = []
    for name, fsrc, bdst, nog in (("wq", wq_f, wq_b, OGD), ("wkv", wkv_f, wkv_b, OGD),
                                  ("wo", wo_f, wo_b, OGD)):
        for g in range(nog):
            cast_order.append((name, g, fsrc, bdst))
    for g in range(OGI):
        cast_order.append(("wg", g, wg_f, wg_b))
        cast_order.append(("wu", g, wu_f, wu_b))
    for g in range(OGDN):
        cast_order.append(("wd", g, wd_f, wd_b))
    cast_at = {}
    for (name, g, fsrc, bdst) in cast_order:
        p.op("gpsimd", lambda e, s=fsrc, d=bdst, gg=g:
             e.dma_start(out=d.ap()[gg], in_=s.ap()[gg]), inc=SC, inc_n=16)
        cast_at[(name, g)] = (SC, SC.count)

    strip_ctr = [0]

    def load_strip(name, bdst, g, width, qd=None):
        sl = strip_ctr[0] % 2
        strip_ctr[0] += 1
        src = bdst.ap()[g] if qd is None else bdst.ap()[g, qd]
        dst = WS[:, sl * SLOTW:sl * SLOTW + width]
        waits = [cast_at[(name, g)]] + rSLOT[sl].require()
        p.op("sync", lambda e, s=src, d=dst: e.dma_start(out=d, in_=s),
             waits=waits, inc=SW, inc_n=16)
        return sl, (SW, SW.count)

    def emit_dbg_taps():
        p.barrier(engines=("scalar",))
        pairs = [("dbg_qbf", QBF[:]), ("dbg_vh", VHBF[:]), ("dbg_xbf", XBF[:]),
                 ("dbg_sqf", SQF[:]), ("dbg_a", A[:]), ("dbg_b", B[:]),
                 ("dbg_hbf", HBF[:]), ("dbg_pref", PREF[:])]
        for k, ap in pairs:
            p.op("scalar", lambda e, d=dbg_out[k], a_=ap:
                 e.dma_start(out=d.ap(), in_=a_), inc=SS, inc_n=16)
        for bb in range(BL):
            for ss0 in range(0, S, T):
                p.op("scalar", lambda e, bb_=bb, s_=ss0:
                     e.dma_start(out=out_fm.ap()[bb_, :, s_:s_ + T].rearrange(
                         "(c p) t -> p c t", p=128),
                         in_=B[:].rearrange("p (c t) -> p c t", c=KD)),
                     inc=SS, inc_n=16)

    def ck(tag, b, ti):
        if _dbg and _dbg.get("stop") == tag and \
                (b * NT + ti) == _dbg.get("ntiles", 8) - 1:
            raise _StopEmit()

    def ln_row_mms(c, src_bf_buf, src_sq_buf, sq_off, rowb0, rowb1, w_c):
        w = [w_c, memset_w] + (rowb0.require() if c == 0 else [])
        p.op("tensor", lambda e, c_=c, st=(c == 0), sp=(c == KD - 1):
             e.matmul(PSB[0:1, 0:T], onescol[:],
                      src_bf_buf[:, c_ * T:(c_ + 1) * T],
                      start=st, stop=sp, skip_group_check=True),
             waits=w, inc=ST if c == KD - 1 else None)
        w = [w_c] + (rowb1.require() if c == 0 else [])
        p.op("tensor", lambda e, c_=c, st=(c == 0), sp=(c == KD - 1):
             e.matmul(PSB[0:1, T:2 * T], onescol[:],
                      src_sq_buf[:, sq_off + c_ * T:sq_off + (c_ + 1) * T],
                      start=st, stop=sp, skip_group_check=True),
             waits=w, inc=ST if c == KD - 1 else None)

    def ln_rows_apply(src_f32, dst_f32, dst_bf,
                      bc_ps, rowb0, rowb1, bcb0, bcb1, store_tile):
        rows_mm = (ST, ST.count)
        p.op("vector", lambda e:
             e.tensor_scalar_mul(ROWSF[0:1, 0:T], PSB[0:1, 0:T], 1.0 / D),
             waits=[rows_mm], inc=SV)
        p.op("vector", lambda e:
             e.tensor_mul(ROWSF[0:1, T:2 * T], ROWSF[0:1, 0:T],
                          ROWSF[0:1, 0:T]), inc=SV)
        p.op("vector", lambda e:
             e.scalar_tensor_tensor(ROWSF[0:1, T:2 * T], PSB[0:1, T:2 * T],
                                    1.0 / D, ROWSF[0:1, T:2 * T],
                                    ALU.mult, ALU.subtract), inc=SV)
        p.op("vector", lambda e:
             e.tensor_scalar_add(ROWSF[0:1, T:2 * T], ROWSF[0:1, T:2 * T], EPS),
             inc=SV)
        p.op("vector", lambda e:
             e.reciprocal(ROWSF[0:1, T:2 * T], ROWSF[0:1, T:2 * T]), inc=SV)
        rowsv = (SV, SV.count)
        rowb0.set([rowsv])
        rowb1.set([rowsv])
        p.op("scalar", lambda e:
             e.activation(ROWSB[0:1, T:2 * T], ROWSF[0:1, T:2 * T], AF.Sqrt),
             waits=[rowsv], inc=SA)
        p.op("scalar", lambda e:
             e.activation(ROWSB[0:1, 0:T], ROWSF[0:1, 0:T], AF.Copy),
             waits=[rowsv], inc=SA)
        rowsb = (SA, SA.count)
        p.op("tensor", lambda e:
             e.matmul(bc_ps[:, 0:T], ones1[:], ROWSB[0:1, 0:T],
                      start=True, stop=True, skip_group_check=True),
             waits=[rowsb, memset_w] + bcb0.require(), inc=ST)
        p.op("tensor", lambda e:
             e.matmul(bc_ps[:, T:2 * T], ones1[:], ROWSB[0:1, T:2 * T],
                      start=True, stop=True, skip_group_check=True),
             waits=[rowsb] + bcb1.require(), inc=ST)
        bc_w = (ST, ST.count)
        apply_ws = []
        for c in range(KD):
            tslot = Ctmp[:, (c % 2) * T:(c % 2 + 1) * T]
            p.op("vector", lambda e, c_=c, ts=tslot:
                 e.tensor_sub(ts, src_f32[:, c_ * T:(c_ + 1) * T], bc_ps[:, 0:T]),
                 waits=[bc_w] if c == 0 else (), inc=SV)
            p.op("vector", lambda e, c_=c, ts=tslot:
                 e.tensor_mul(dst_f32[:, c_ * T:(c_ + 1) * T], ts,
                              bc_ps[:, T:2 * T]), inc=SV)
            app_w = (SV, SV.count)
            if dst_bf is not None:
                p.op("scalar", lambda e, c_=c:
                     e.activation(dst_bf[:, c_ * T:(c_ + 1) * T],
                                  dst_f32[:, c_ * T:(c_ + 1) * T], AF.Copy),
                     waits=[app_w], inc=SA)
                apply_ws.append((SA, SA.count))
            if store_tile is not None:
                bb, ss0 = store_tile
                p.op("scalar", lambda e, c_=c:
                     e.dma_start(out=out_fm.ap()[bb, c_ * 128:(c_ + 1) * 128,
                                                 ss0:ss0 + T],
                                 in_=dst_f32[:, c_ * T:(c_ + 1) * T]),
                     waits=[app_w], inc=SS, inc_n=16)
        lastv = (SV, SV.count)
        bcb0.set([lastv])
        bcb1.set([lastv])
        return apply_ws, lastv

    first_tile = True
    pref_w = {}
    rowb0 = Region()
    rowb1 = Region()
    rD0 = Region()
    rD1 = Region()
    prev_psa = []
    prev_psc = []
    try:
        for b in range(BL):
            for ti in range(NT):
                s0 = ti * T
                if not first_tile:
                    emit_xbf_load(b, s0, rXBF.require())
                    emit_xf_load(b, s0, rB_F32.require())
                first_tile = False
                xbf_w = (SG, SG.count)
                xf_w = (SS, SS.count)

                if ti == 0:
                    p.op("vector", lambda e: e.memset(PREF[:], 0.0),
                         waits=[(SA, SA.count)], inc=SV)
                    pref_w = {}

                def proj_qkv(name, bdst, dst_bf, maskcol, sq_extra,
                             seed0, seed1, mask_reg):
                    pair_r = [Region(), Region()]
                    pair_r[0].set(seed0)
                    pair_r[1].set(seed1)
                    ev_last = None
                    for g in range(OGD):
                        sl, w_w = load_strip(name, bdst, g, KD * 256)
                        ps = PSA if g % 2 == 0 else PSC
                        waits = [w_w, xbf_w] + pair_r[g % 2].require()
                        if g == 0:
                            waits += [consts_w]
                        for k in range(KD):
                            for oc in range(2):
                                lhsT = WS[:, sl * SLOTW + k * 256 + oc * 128:
                                          sl * SLOTW + k * 256 + (oc + 1) * 128]
                                p.op("tensor",
                                     lambda e, o=ps[:, oc * T:(oc + 1) * T],
                                     l=lhsT, r=XBF[:, k * T:(k + 1) * T],
                                     st=(k == 0), sp=(k == KD - 1):
                                     e.matmul(o, l, r, start=st, stop=sp,
                                              skip_group_check=True),
                                     waits=waits if (k == 0 and oc == 0) else (),
                                     inc=ST if (k == KD - 1 and oc == 1) else None)
                        mm_w = (ST, ST.count)
                        rSLOT[sl].set([mm_w])
                        cb = 2 * g
                        p.op("vector", lambda e, cb_=cb, ps_=ps:
                             e.tensor_copy(dst_bf[:, cb_ * T:(cb_ + 2) * T],
                                           ps_[:, 0:1024]),
                             waits=[mm_w], inc=SV)
                        ev_w = (SV, SV.count)
                        sqw = [ev_w] + (sq_extra if g == 0 else [])
                        p.op("scalar", lambda e, cb_=cb:
                             e.activation(HBF[:, SH + cb_ * T:SH + (cb_ + 2) * T],
                                          dst_bf[:, cb_ * T:(cb_ + 2) * T],
                                          AF.Square),
                             waits=sqw, inc=SA)
                        pair_r[g % 2].set([ev_w])
                        ev_last = ev_w
                    sq_w = (SA, SA.count)
                    mwaits = [sq_w, consts_w] + mask_reg.require()
                    for hh in range(16):
                        p.op("tensor", lambda e, h_=hh, st=(hh == 0), sp=(hh == 15):
                             e.matmul(PSB[0:16, maskcol * T:(maskcol + 1) * T],
                                      maskA[:, h_ * 16:(h_ + 1) * 16],
                                      HBF[:, SH + h_ * T:SH + (h_ + 1) * T],
                                      start=st, stop=sp, skip_group_check=True),
                             waits=mwaits if hh == 0 else (),
                             inc=ST if hh == 15 else None)
                    return (ST, ST.count), ev_last, pair_r

                ssq_w, _, qpair = proj_qkv("wq", wq_b, QBF, 0, [],
                                           prev_psa, prev_psc, rowb0)
                ck("Aq", b, ti)
                sskv_w, kv_ev, kvpair = proj_qkv(
                    "wkv", wkv_b, HBF, 1, [ssq_w],
                    qpair[0].require(), qpair[1].require(), rowb1)

                p.op("vector", lambda e:
                     e.reciprocal(SQF[:, 0:2 * T], PSB[0:16, 0:2 * T]),
                     waits=[ssq_w, sskv_w], inc=SV)
                rec_w = (SV, SV.count)
                p.op("scalar", lambda e:
                     e.activation(SQB[:, 0:T], SQF[:, 0:T], AF.Sqrt,
                                  scale=1.0 / float(HD) ** 4),
                     waits=[rec_w], inc=SA)
                p.op("scalar", lambda e:
                     e.activation(SQB[:, T:2 * T], SQF[:, T:2 * T], AF.Sqrt),
                     waits=[rec_w], inc=SA)
                rows_w = (SA, SA.count)
                ck("Arows", b, ti)

                pref_add_t = {}
                mixtr_w = sskv_w
                rB0 = Region(); rB0.set([rec_w])
                rA0 = Region(); rA0.set(qpair[0].require() + kvpair[0].require())
                rA1 = Region(); rA1.set(rA0.require())
                rC0 = Region(); rC0.set(qpair[1].require() + kvpair[1].require())
                rC1 = Region(); rC1.set(rC0.require())
                vh_w_head = []
                ktb_h = {}
                scan_h = {}
                red_h = {}
                kvn_aps = {}
                prodsem = {}
                vh_mul_h = {}
                vh_mm_h = {}
                qcb_h = {}
                qsb_h = {}
                ksb_h = {}

                for s_ in range(17):
                    if s_ < 16:
                        h = s_
                        rot = h % 2
                        p.op("tensor", lambda e, h_=h:
                             e.matmul(PSB[:, 0:T],
                                      maskB[:, h_ * 128:(h_ + 1) * 128],
                                      SQB[:, T:2 * T], start=True, stop=True,
                                      skip_group_check=True),
                             waits=[rows_w] + rB0.require(), inc=ST)
                        bckv_w = (ST, ST.count)
                        kvn = HBF[:, M_KVN + rot * T:M_KVN + (rot + 1) * T]
                        kvn_aps[h] = kvn
                        p.op("vector", lambda e, h_=h, o=kvn:
                             e.tensor_mul(o, HBF[:, h_ * T:(h_ + 1) * T],
                                          PSB[:, 0:T]),
                             waits=[bckv_w, red_h.get(h - 2)] +
                                   ([mixtr_w] if h < 2 else []), inc=SV)
                        rB0.set([(SV, SV.count)])
                        kt = Ctmp[:, rot * T:(rot + 1) * T]
                        p.op("vector", lambda e, o=kt, i=kvn, h_=h:
                             e.tensor_tensor_scan(o, onest[:], i,
                                                  PREF[:, h_:h_ + 1],
                                                  ALU.mult, ALU.add),
                             waits=[memset_w, pref_w.get(h), ktb_h.get(h - 2)],
                             inc=SV)
                        scan_h[h] = (SV, SV.count)
                        p.op("vector", lambda e, i=kvn, h_=h:
                             e.reduce_sum(PREFT[:, h_ % 4:h_ % 4 + 1], i,
                                          axis=AX.X),
                             waits=[pref_add_t.get(h - 4)], inc=SV)
                        red_h[h] = (SV, SV.count)
                        p.op("scalar", lambda e, h_=h:
                             e.add(PREF[:, h_:h_ + 1], PREF[:, h_:h_ + 1],
                                   PREFT[:, h_ % 4:h_ % 4 + 1]),
                             waits=[red_h[h]], inc=SA)
                        pref_w[h] = (SA, SA.count)
                        pref_add_t[h] = (SA, SA.count)
                        ktb = HBF[:, M_KTB + rot * T:M_KTB + (rot + 1) * T]
                        p.op("scalar", lambda e, o=ktb, i=kt:
                             e.activation(o, i, AF.Copy),
                             waits=[scan_h[h]], inc=SA)
                        ktb_h[h] = (SA, SA.count)
                    if s_ >= 1:
                        h = s_ - 1
                        ktb = HBF[:, M_KTB + (h % 2) * T:M_KTB + (h % 2 + 1) * T]
                        p.op("tensor", lambda e, h_=h:
                             e.matmul(PSC[:, 0:T], cdft[:],
                                      QBF[:, h_ * T:(h_ + 1) * T],
                                      start=True, stop=True,
                                      skip_group_check=True),
                             waits=rC0.require() + ([mixtr_w] if h == 0 else []),
                             inc=ST)
                        qc_w = (ST, ST.count)
                        p.op("tensor", lambda e, h_=h:
                             e.matmul(PSC[:, T:2 * T], sdft[:],
                                      QBF[:, h_ * T:(h_ + 1) * T],
                                      start=True, stop=True,
                                      skip_group_check=True),
                             waits=rC1.require(), inc=ST)
                        qs_w = (ST, ST.count)
                        p.op("tensor", lambda e, i=ktb:
                             e.matmul(PSD[:, 0:T], cdft[:], i, start=True,
                                      stop=True, skip_group_check=True),
                             waits=[ktb_h[h]] + rD0.require(), inc=ST)
                        kc_w = (ST, ST.count)
                        p.op("tensor", lambda e, i=ktb:
                             e.matmul(PSD[:, T:2 * T], sdft[:], i, start=True,
                                      stop=True, skip_group_check=True),
                             waits=rD1.require(), inc=ST)
                        ks_w = (ST, ST.count)

                        dbase = M_DFT + (h % 2) * 4 * T
                        qcb = HBF[:, dbase:dbase + T]
                        qsb = HBF[:, dbase + T:dbase + 2 * T]
                        kcb = HBF[:, dbase + 2 * T:dbase + 3 * T]
                        ksb = HBF[:, dbase + 3 * T:dbase + 4 * T]
                        pg = prodsem.get(h - 2, {})
                        p.op("vector", lambda e, o=qcb:
                             e.tensor_copy(o, PSC[:, 0:T]),
                             waits=[qc_w, pg.get(3)], inc=SV)
                        rC0.set([(SV, SV.count)])
                        qcb_h[h] = (SV, SV.count)
                        p.op("vector", lambda e, o=kcb:
                             e.tensor_copy(o, PSD[:, 0:T]),
                             waits=[kc_w], inc=SV)
                        rD0.set([(SV, SV.count)])
                        kcb_h = (SV, SV.count)
                        p.op("scalar", lambda e, o=qsb:
                             e.activation(o, PSC[:, T:2 * T], AF.Copy),
                             waits=[qs_w, pg.get(2), pg.get(4)], inc=SA)
                        rC1.set([(SA, SA.count)])
                        qsb_h[h] = (SA, SA.count)
                        p.op("scalar", lambda e, o=ksb:
                             e.activation(o, PSD[:, T:2 * T], AF.Copy),
                             waits=[ks_w, pg.get(2), pg.get(3)], inc=SA)
                        rD1.set([(SA, SA.count)])
                        ksb_h[h] = (SA, SA.count)

                        pbase = M_P + (h % 2) * 4 * T
                        prods = [HBF[:, pbase + j * T:pbase + (j + 1) * T]
                                 for j in range(4)]
                        slot_free = vh_mm_h.get(h - 2)
                        ps_ = {}
                        p.op("vector", lambda e, o=prods[0], a_=qcb, b_=kcb:
                             e.tensor_mul(o, a_, b_),
                             waits=[slot_free], inc=SV)
                        ps_[1] = (SV, SV.count)
                        p.op("gpsimd", lambda e, o=prods[1], a_=qsb, b_=ksb:
                             e.tensor_mul(o, a_, b_),
                             waits=[qsb_h[h], ksb_h[h], slot_free], inc=SP)
                        ps_[2] = (SP, SP.count)
                        p.op("vector", lambda e, o=prods[2], a_=qcb, b_=ksb:
                             e.tensor_mul(o, a_, b_),
                             waits=[ksb_h[h], slot_free], inc=SV)
                        ps_[3] = (SV, SV.count)
                        p.op("vector", lambda e, o=prods[3], a_=qsb, b_=kcb:
                             e.tensor_mul(o, a_, b_),
                             waits=[qsb_h[h], slot_free], inc=SV)
                        ps_[4] = (SV, SV.count)
                        prodsem[h] = ps_
                        lhs_list = [cdft, ncdft, sdft, sdft]
                        pws = [ps_[1], ps_[2], ps_[3], ps_[4]]
                        for j in range(4):
                            p.op("tensor", lambda e, l=lhs_list[j], r=prods[j],
                                 st=(j == 0), sp=(j == 3):
                                 e.matmul(PSA[:, 0:T], l[:], r, start=st,
                                          stop=sp, skip_group_check=True),
                                 waits=[pws[j]] +
                                       (rA0.require() if j == 0 else []),
                                 inc=ST if j == 3 else None)
                        vh_w = (ST, ST.count)
                        vh_mm_h[h] = vh_w
                        p.op("tensor", lambda e, h_=h:
                             e.matmul(PSA[:, T:2 * T],
                                      maskB[:, h_ * 128:(h_ + 1) * 128],
                                      SQB[:, 0:T], start=True, stop=True,
                                      skip_group_check=True),
                             waits=[rows_w] + rA1.require(), inc=ST)
                        bcq_w = (ST, ST.count)
                        sqqb = HBF[:, M_SQ + (h % 2) * T:M_SQ + (h % 2 + 1) * T]
                        p.op("scalar", lambda e, o=sqqb:
                             e.activation(o, PSA[:, T:2 * T], AF.Copy),
                             waits=[bcq_w, vh_mul_h.get(h - 2)], inc=SA)
                        sqqb_w = (SA, SA.count)
                        rA1.set([sqqb_w])
                        p.op("vector", lambda e, h_=h, s_a=sqqb:
                             e.tensor_mul(VHBF[:, h_ * T:(h_ + 1) * T],
                                          PSA[:, 0:T], s_a),
                             waits=[vh_w, sqqb_w], inc=SV)
                        rA0.set([(SV, SV.count)])
                        vh_mul_h[h] = (SV, SV.count)
                        vh_w_head.append((SV, SV.count))
                        ck(f"Ah{h}", b, ti)
                vh_done = (SV, SV.count)
                ck("A", b, ti)

                rowb0.set(rB0.require())
                rowb1.set([rec_w])
                pair_r = [Region(), Region()]
                pair_r[0].set(rA0.require() + rA1.require())
                pair_r[1].set(rC0.require() + rC1.require())
                x2a_ws = []
                for g in range(OGD):
                    sl, w_w = load_strip("wo", wo_b, g, KD * 256)
                    ps = PSA if g % 2 == 0 else PSC
                    waits = [w_w] + pair_r[g % 2].require()
                    for k in range(KD):
                        if g < 2:
                            kw = (waits + [vh_w_head[k]]) if k == 0 else \
                                [vh_w_head[k]]
                        else:
                            kw = waits if k == 0 else ()
                        for oc in range(2):
                            lhsT = WS[:, sl * SLOTW + k * 256 + oc * 128:
                                      sl * SLOTW + k * 256 + (oc + 1) * 128]
                            p.op("tensor",
                                 lambda e, o=ps[:, oc * T:(oc + 1) * T],
                                 l=lhsT, r=VHBF[:, k * T:(k + 1) * T],
                                 st=(k == 0), sp=(k == KD - 1):
                                 e.matmul(o, l, r, start=st, stop=sp,
                                          skip_group_check=True),
                                 waits=kw if oc == 0 else (),
                                 inc=ST if (k == KD - 1 and oc == 1) else None)
                    mm_w = (ST, ST.count)
                    rSLOT[sl].set([mm_w])
                    cb = 2 * g
                    p.op("vector", lambda e, cb_=cb, ps_=ps:
                         e.tensor_add(A[:, cb_ * T:(cb_ + 2) * T], ps_[:, 0:1024],
                                      B[:, cb_ * T:(cb_ + 2) * T]),
                         waits=[mm_w, xf_w], inc=SV)
                    x2a_w = (SV, SV.count)
                    pair_r[g % 2].set([x2a_w])
                    p.op("scalar", lambda e, cb_=cb:
                         e.activation(HBF[:, cb_ * T:(cb_ + 2) * T],
                                      A[:, cb_ * T:(cb_ + 2) * T], AF.Copy),
                         waits=[x2a_w] + ([vh_done] if g == 0 else []), inc=SA)
                    p.op("scalar", lambda e, cb_=cb:
                         e.activation(HBF[:, SH + cb_ * T:SH + (cb_ + 2) * T],
                                      A[:, cb_ * T:(cb_ + 2) * T], AF.Square),
                         inc=SA)
                    w2 = (SA, SA.count)
                    x2a_ws.extend([w2, w2])
                    ln_row_mms(cb, HBF, HBF, SH, rowb0, rowb1, w2)
                    ln_row_mms(cb + 1, HBF, HBF, SH, rowb0, rowb1, w2)

                x1_ws, ln1_v = ln_rows_apply(
                    A, B, XBF, PSD, rowb0, rowb1, rD0, rD1, None)
                rB_F32.set([])
                ck("B", b, ti)

                hprev = {}
                h_ws = []
                gpair = [Region(), Region()]
                gpair[0].set(pair_r[0].require())
                gpair[1].set(pair_r[1].require())
                upair = [Region(), Region()]
                upair[0].set(rowb0.require() + rowb1.require())
                upair[1].set([ln1_v])
                for g in range(OGI):
                    slg, wg_w = load_strip("wg", wg_b, g, KD * 256)
                    psg = PSA if g % 2 == 0 else PSC
                    waits = [wg_w] + gpair[g % 2].require()
                    for k in range(KD):
                        if g < 2:
                            kw = (waits + [x1_ws[k]]) if k == 0 else [x1_ws[k]]
                        else:
                            kw = waits if k == 0 else ()
                        for oc in range(2):
                            lhsT = WS[:, slg * SLOTW + k * 256 + oc * 128:
                                      slg * SLOTW + k * 256 + (oc + 1) * 128]
                            p.op("tensor",
                                 lambda e, o=psg[:, oc * T:(oc + 1) * T],
                                 l=lhsT, r=XBF[:, k * T:(k + 1) * T],
                                 st=(k == 0), sp=(k == KD - 1):
                                 e.matmul(o, l, r, start=st, stop=sp,
                                          skip_group_check=True),
                                 waits=kw if oc == 0 else (),
                                 inc=ST if (k == KD - 1 and oc == 1) else None)
                    g_w = (ST, ST.count)
                    rSLOT[slg].set([g_w])
                    slu, wu_w = load_strip("wu", wu_b, g, KD * 256)
                    psu = PSB if g % 2 == 0 else PSD
                    waitsu = [wu_w] + upair[g % 2].require()
                    for k in range(KD):
                        if g < 2:
                            kw = (waitsu + [x1_ws[k]]) if k == 0 else [x1_ws[k]]
                        else:
                            kw = waitsu if k == 0 else ()
                        for oc in range(2):
                            lhsT = WS[:, slu * SLOTW + k * 256 + oc * 128:
                                      slu * SLOTW + k * 256 + (oc + 1) * 128]
                            p.op("tensor",
                                 lambda e, o=psu[:, oc * T:(oc + 1) * T],
                                 l=lhsT, r=XBF[:, k * T:(k + 1) * T],
                                 st=(k == 0), sp=(k == KD - 1):
                                 e.matmul(o, l, r, start=st, stop=sp,
                                          skip_group_check=True),
                                 waits=kw if oc == 0 else (),
                                 inc=ST if (k == KD - 1 and oc == 1) else None)
                    u_w = (ST, ST.count)
                    rSLOT[slu].set([u_w])
                    for oc in range(2):
                        c = 2 * g + oc
                        rot = c % 2
                        sg_t = QBF[:, rot * T:(rot + 1) * T]
                        u_t = QBF[:, (2 + rot) * T:(3 + rot) * T]
                        hw = [hprev[rot]] if c >= 2 else []
                        p.op("scalar", lambda e, o=sg_t, oc_=oc, c_=c, ps_=psg:
                             e.activation(o, ps_[:, oc_ * T:(oc_ + 1) * T],
                                          AF.Silu, scale=gs[:, c_:c_ + 1]),
                             waits=[g_w] + hw, inc=SA)
                        p.op("scalar", lambda e, o=u_t, oc_=oc, c_=c, ps_=psu:
                             e.mul(o, ps_[:, oc_ * T:(oc_ + 1) * T],
                                   us[:, c_:c_ + 1]),
                             waits=[u_w], inc=SA)
                        su_w = (SA, SA.count)
                        p.op("vector", lambda e, c_=c, a_=sg_t, b_=u_t:
                             e.tensor_mul(HBF[:, c_ * T:(c_ + 1) * T], a_, b_),
                             waits=[su_w], inc=SV)
                        hprev[rot] = (SV, SV.count)
                        h_ws.append((SV, SV.count))
                    gpair[g % 2].set([(SA, SA.count)])
                    upair[g % 2].set([(SA, SA.count)])
                rXBF.set([(ST, ST.count)])
                ck("C", b, ti)

                rowb0.set(upair[0].require())
                rowb1.set(upair[0].require())
                bank_regs = [Region(), Region(), Region(), Region()]
                bank_regs[0].set(gpair[0].require())
                bank_regs[1].set(gpair[0].require())
                bank_regs[2].set(gpair[1].require())
                bank_regs[3].set(gpair[1].require())
                bank_aps = [(PSA, 0), (PSA, 1), (PSC, 0), (PSC, 1)]
                x2b_ws = []
                rQ = [Region(), Region(), Region(), Region()]
                rQ[0].set(rSLOT[0].require())
                rQ[1].set(rSLOT[0].require())
                rQ[2].set(rSLOT[1].require())
                rQ[3].set(rSLOT[1].require())
                qbase = [0, 2048, SLOTW, SLOTW + 2048]
                qctr = [0]
                qmap = {}

                def load_q(g_, qd):
                    ss = qctr[0] % 4
                    qctr[0] += 1
                    dst = WS[:, qbase[ss]:qbase[ss] + KH * 128]
                    waits = [cast_at[("wd", g_)]] + rQ[ss].require()
                    p.op("sync", lambda e, s=wd_b.ap()[g_, qd], d=dst:
                         e.dma_start(out=d, in_=s),
                         waits=waits, inc=SW, inc_n=16)
                    qmap[(g_, qd)] = (ss, (SW, SW.count))

                for qd in range(4):
                    load_q(0, qd)
                for g in range(OGDN):
                    ps, half = bank_aps[g % 4]
                    waits = [qmap[(g, 0)][1]] + bank_regs[g % 4].require()
                    for k in range(KI):
                        qd = k // KH
                        lastq = (k % KH == KH - 1)
                        ss, q_w = qmap[(g, qd)]
                        lhsT = WS[:, qbase[ss] + (k % KH) * 128:
                                  qbase[ss] + (k % KH + 1) * 128]
                        kw = list(waits) if k == 0 else \
                            ([q_w] if k % KH == 0 else [])
                        if g < 4:
                            kw = kw + [h_ws[k]]
                        p.op("tensor",
                             lambda e, o=ps[:, half * T:(half + 1) * T],
                             l=lhsT, k_=k, st=(k == 0), sp=(k == KI - 1):
                             e.matmul(o, l, HBF[:, k_ * T:(k_ + 1) * T],
                                      start=st, stop=sp, skip_group_check=True),
                             waits=kw, inc=ST if lastq else None)
                        if lastq:
                            rQ[ss].set([(ST, ST.count)])
                            nxt = g * 4 + qd + 4
                            if nxt < OGDN * 4:
                                load_q(nxt // 4, nxt % 4)
                    mm_w = (ST, ST.count)
                    c = g
                    p.op("vector", lambda e, c_=c, o=A[:, c * T:(c + 1) * T],
                         i=ps[:, half * T:(half + 1) * T]:
                         e.scalar_tensor_tensor(o, i, ds[:, c_:c_ + 1],
                                                B[:, c_ * T:(c_ + 1) * T],
                                                ALU.mult, ALU.add),
                         waits=[mm_w], inc=SV)
                    x2b_w = (SV, SV.count)
                    bank_regs[g % 4].set([x2b_w])
                    p.op("scalar", lambda e, c_=c:
                         e.activation(VHBF[:, c_ * T:(c_ + 1) * T],
                                      A[:, c_ * T:(c_ + 1) * T], AF.Copy),
                         waits=[x2b_w], inc=SA)
                    p.op("scalar", lambda e, c_=c:
                         e.activation(QBF[:, c_ * T:(c_ + 1) * T],
                                      A[:, c_ * T:(c_ + 1) * T], AF.Square),
                         waits=[hprev[0], hprev[1]] if g == 0 else (), inc=SA)
                    w2 = (SA, SA.count)
                    x2b_ws.append(w2)
                    ln_row_mms(c, VHBF, QBF, 0, rowb0, rowb1, w2)

                rD0.set(upair[1].require())
                rD1.set(upair[1].require())
                _, ln2_v = ln_rows_apply(
                    A, B, None, PSD, rowb0, rowb1, rD0, rD1, (b, s0))
                store_done = (SS, SS.count)
                rB_F32.set([store_done])
                rSLOT[0].set([mm_w])
                rSLOT[1].set([mm_w])
                prev_psa = bank_regs[0].require() + bank_regs[1].require()
                prev_psc = bank_regs[2].require() + bank_regs[3].require()
                ck("D", b, ti)
                p.barrier(only=("s_t", "s_v", "s_a", "s_p"))
    except _StopEmit:
        pass
    if _dbg:
        emit_dbg_taps()

    def fin(eng):
        eng.wait_ge(SS.h, SS.count)
    p.streams["scalar"].append(fin)
    p.finalize()
    return nc


def _prep_host(inputs):
    x = np.ascontiguousarray(np.asarray(inputs["L0_attn_x"], dtype=np.float32))
    wq = np.asarray(inputs["L0_attn_W_q"], dtype=np.float32)
    wkv = np.asarray(inputs["L0_attn_W_kv"], dtype=np.float32)
    wo = np.asarray(inputs["L0_attn_W_o"], dtype=np.float32)
    gw = np.asarray(inputs["L0_ffn_gate_w"], dtype=np.float32)
    uw = np.asarray(inputs["L0_ffn_up_w"], dtype=np.float32)
    dw = np.asarray(inputs["L0_ffn_down_w"], dtype=np.float32)
    gsv = np.asarray(inputs["L0_ffn_gate_s"], dtype=np.float32)
    usv = np.asarray(inputs["L0_ffn_up_s"], dtype=np.float32)
    dsv = np.asarray(inputs["L0_ffn_down_s"], dtype=np.float32)

    def strip_d(w, nog):
        return np.ascontiguousarray(
            w.T.reshape(KD, 128, nog, 256).transpose(2, 1, 0, 3).reshape(
                nog, 128, KD * 256))

    wd_s = np.ascontiguousarray(
        dw.T.reshape(4, KH, 128, OGDN, 128).transpose(3, 0, 2, 1, 4).reshape(
            OGDN, 4, 128, KH * 128))

    m = np.arange(128)
    ang = 2.0 * np.pi * np.outer(m, m) / 128.0
    cdft = np.cos(ang).astype(np.float32).astype(ml_dtypes.bfloat16)
    sdft = np.sin(ang).astype(np.float32).astype(ml_dtypes.bfloat16)
    ncdft = (-np.cos(ang)).astype(np.float32).astype(ml_dtypes.bfloat16)
    maskA = np.zeros((128, 16, 16), dtype=ml_dtypes.bfloat16)
    for h in range(16):
        maskA[:, h, h] = 1
    maskB = np.zeros((16, 16, 128), dtype=ml_dtypes.bfloat16)
    for h in range(16):
        maskB[h, h, :] = 1

    shared = dict(
        wq_f=strip_d(wq, OGD), wkv_f=strip_d(wkv, OGD), wo_f=strip_d(wo, OGD),
        wg_f=strip_d(gw, OGI), wu_f=strip_d(uw, OGI), wd_f=wd_s,
        gs_in=np.ascontiguousarray(gsv.reshape(KI, 128).T),
        us_in=np.ascontiguousarray(usv.reshape(KI, 128).T),
        ds_in=np.ascontiguousarray(dsv.reshape(KD, 128).T),
        cdft_in=cdft, sdft_in=sdft, ncdft_in=ncdft,
        maskA_in=np.ascontiguousarray(maskA.reshape(128, 256)),
        maskB_in=np.ascontiguousarray(maskB.reshape(16, 2048)),
    )
    in_maps = []
    for i in range(NCORES):
        xc = np.ascontiguousarray(x[BL * i:BL * (i + 1)].transpose(0, 2, 1))
        im = dict(shared)
        im["x_fm"] = xc
        in_maps.append(im)
    return in_maps


def kernel(**inputs):
    in_maps = _prep_host(inputs)
    nc = build_nc()
    res = run_bass_kernel_spmd(nc, in_maps, core_ids=list(range(NCORES)))
    out = np.empty((16, S, D), dtype=np.float32)
    for i in range(NCORES):
        o = np.asarray(res.results[i]["out_fm"])
        out[BL * i:BL * (i + 1)] = o.transpose(0, 2, 1)
    return out


# version 15
# speedup vs baseline: 1.1226x; 1.1226x over previous
import numpy as np
import ml_dtypes
import concourse.bass as bass
import concourse.mybir as mybir
from concourse.bass_utils import run_bass_kernel_spmd

F32 = mybir.dt.float32
BF16 = mybir.dt.bfloat16
AF = mybir.ActivationFunctionType
ALU = mybir.AluOpType
AX = mybir.AxisListType

D = 2048
S = 2048
H = 16
HD = 128
INTER = 5632
T = 512
NT = S // T
BL = 2
KD = D // 128
KI = INTER // 128
KH = KI // 4
OGD = 8
OGI = 22
OGDN = 16
NCORES = 8
EPS = 1e-5
SLOTW = 4096


class Sem:
    def __init__(self, nc, name):
        self.h = nc.alloc_semaphore(name)
        self.count = 0
        self.name = name


class Region:
    def __init__(self):
        self.req = []

    def require(self):
        return list(self.req)

    def set(self, reqs):
        self.req = [r for r in reqs if r is not None]


class Plan:
    ENGINES = ("tensor", "vector", "scalar", "sync", "gpsimd")

    def __init__(self, nc):
        self.nc = nc
        self.streams = {e: [] for e in self.ENGINES}
        self.sems = []

    def sem(self, name):
        s = Sem(self.nc, name)
        self.sems.append(s)
        return s

    def op(self, engine, build, waits=(), inc=None, inc_n=1):
        wmap = {}
        for w in waits:
            if w is None:
                continue
            s, v = w
            assert v <= s.count, f"wait on {s.name} for {v} > emitted {s.count}"
            if v > 0 and (s.h not in wmap or wmap[s.h] < v):
                wmap[s.h] = v
        if inc is not None:
            inc.count += inc_n
            inc_h = inc.h
        else:
            inc_h = None
        wlist = list(wmap.items())

        def fn(eng, _wl=wlist, _build=build, _ih=inc_h, _n=inc_n):
            for h, v in _wl:
                eng.wait_ge(h, v)
            inst = _build(eng)
            if _ih is not None:
                inst.then_inc(_ih, _n)
        self.streams[engine].append(fn)

    def barrier(self, engines=("tensor", "vector", "scalar", "gpsimd"),
                only=None):
        snap = [(s.h, s.count) for s in self.sems
                if s.count > 0 and (only is None or s.name in only)]
        for e in engines:
            def fn(eng, _snap=list(snap)):
                for h, v in _snap:
                    eng.wait_ge(h, v)
            self.streams[e].append(fn)

    def finalize(self):
        nc = self.nc
        with nc.Block() as block:
            block.tensor(lambda t: [f(t) for f in self.streams["tensor"]])
            block.vector(lambda v: [f(v) for f in self.streams["vector"]])
            block.scalar(lambda s: [f(s) for f in self.streams["scalar"]])
            block.sync(lambda s: [f(s) for f in self.streams["sync"]])
            block.gpsimd(lambda g: [f(g) for f in self.streams["gpsimd"]])


class _StopEmit(Exception):
    pass


def build_nc(_dbg=None):
    nc = bass.Bass("TRN2", target_bir_lowering=False)

    x_fm = nc.dram_tensor("x_fm", [BL, D, S], F32, kind="ExternalInput")
    wq_f = nc.dram_tensor("wq_f", [OGD, 128, KD * 256], F32, kind="ExternalInput")
    wkv_f = nc.dram_tensor("wkv_f", [OGD, 128, KD * 256], F32, kind="ExternalInput")
    wo_f = nc.dram_tensor("wo_f", [OGD, 128, KD * 256], F32, kind="ExternalInput")
    wg_f = nc.dram_tensor("wg_f", [OGI, 128, KD * 256], F32, kind="ExternalInput")
    wu_f = nc.dram_tensor("wu_f", [OGI, 128, KD * 256], F32, kind="ExternalInput")
    wd_f = nc.dram_tensor("wd_f", [OGDN, 4, 128, KH * 128], F32, kind="ExternalInput")
    gs_in = nc.dram_tensor("gs_in", [128, KI], F32, kind="ExternalInput")
    us_in = nc.dram_tensor("us_in", [128, KI], F32, kind="ExternalInput")
    ds_in = nc.dram_tensor("ds_in", [128, KD], F32, kind="ExternalInput")
    cdft_in = nc.dram_tensor("cdft_in", [128, 128], BF16, kind="ExternalInput")
    sdft_in = nc.dram_tensor("sdft_in", [128, 128], BF16, kind="ExternalInput")
    ncdft_in = nc.dram_tensor("ncdft_in", [128, 128], BF16, kind="ExternalInput")
    maskA_in = nc.dram_tensor("maskA_in", [128, 256], BF16, kind="ExternalInput")
    maskB_in = nc.dram_tensor("maskB_in", [16, 2048], BF16, kind="ExternalInput")
    out_fm = nc.dram_tensor("out_fm", [BL, D, S], F32, kind="ExternalOutput")
    dbg_out = None
    if _dbg:
        dbg_t = {
            "dbg_qbf": ([128, 8192], BF16), "dbg_vh": ([128, 8192], BF16),
            "dbg_xbf": ([128, 8192], BF16), "dbg_sqf": ([16, 1024], F32),
            "dbg_a": ([128, 8192], F32), "dbg_b": ([128, 8192], F32),
            "dbg_hbf": ([128, 22528], BF16), "dbg_pref": ([128, 16], F32),
        }
        dbg_out = {k: nc.dram_tensor(k, sh, dt, kind="ExternalOutput")
                   for k, (sh, dt) in dbg_t.items()}

    wq_b = nc.dram_tensor("wq_b", [OGD, 128, KD * 256], BF16, kind="Internal")
    wkv_b = nc.dram_tensor("wkv_b", [OGD, 128, KD * 256], BF16, kind="Internal")
    wo_b = nc.dram_tensor("wo_b", [OGD, 128, KD * 256], BF16, kind="Internal")
    wg_b = nc.dram_tensor("wg_b", [OGI, 128, KD * 256], BF16, kind="Internal")
    wu_b = nc.dram_tensor("wu_b", [OGI, 128, KD * 256], BF16, kind="Internal")
    wd_b = nc.dram_tensor("wd_b", [OGDN, 4, 128, KH * 128], BF16, kind="Internal")

    A = nc.alloc_sbuf_tensor("buf_a", [128, KD * T], F32)
    B = nc.alloc_sbuf_tensor("buf_b", [128, KD * T], F32)
    Ctmp = nc.alloc_sbuf_tensor("buf_c", [128, 2 * T], F32)
    SQF = nc.alloc_sbuf_tensor("sq_f", [16, 2 * T], F32)
    SQB = nc.alloc_sbuf_tensor("sq_b", [16, 2 * T], BF16)
    ROWSF = SQF
    ROWSB = SQB
    PREF = nc.alloc_sbuf_tensor("prefix", [128, 16], F32)
    PREFT = nc.alloc_sbuf_tensor("preft", [128, 4], F32)
    XBF = nc.alloc_sbuf_tensor("x_bf", [128, KD * T], BF16)
    QBF = nc.alloc_sbuf_tensor("q_bf", [128, KD * T], BF16)
    VHBF = nc.alloc_sbuf_tensor("vh_bf", [128, KD * T], BF16)
    HBF = nc.alloc_sbuf_tensor("h_bf", [128, KI * T], BF16)
    WS = nc.alloc_sbuf_tensor("wslot", [128, 2 * SLOTW], BF16)
    cdft = nc.alloc_sbuf_tensor("cdft", [128, 128], BF16)
    sdft = nc.alloc_sbuf_tensor("sdft", [128, 128], BF16)
    ncdft = nc.alloc_sbuf_tensor("ncdft", [128, 128], BF16)
    maskA = nc.alloc_sbuf_tensor("maskA", [128, 256], BF16)
    maskB = nc.alloc_sbuf_tensor("maskB", [16, 2048], BF16)
    gs = nc.alloc_sbuf_tensor("gs", [128, KI], F32)
    us = nc.alloc_sbuf_tensor("us", [128, KI], F32)
    ds = nc.alloc_sbuf_tensor("ds", [128, KD], F32)
    ones1 = nc.alloc_sbuf_tensor("ones1", [1, 128], BF16)
    onescol = nc.alloc_sbuf_tensor("onescol", [128, 1], BF16)
    onest = nc.alloc_sbuf_tensor("onest", [128, T], BF16)

    SH = KD * T
    M_QC = SH + 0
    M_QS = SH + T
    M_KC = SH + 2 * T
    M_KS = SH + 3 * T
    M_P = SH + 4 * T
    M_KVN = SH + 12 * T
    M_KTB = SH + 14 * T
    M_SQ = SH + 16 * T
    M_DFT = SH + 18 * T

    PSA = nc.alloc_psum_tensor("psa", [128, 1024], F32)
    PSB = nc.alloc_psum_tensor("psb", [128, 1024], F32)
    PSC = nc.alloc_psum_tensor("psc", [128, 1024], F32)
    PSD = nc.alloc_psum_tensor("psd", [128, 1024], F32)

    p = Plan(nc)
    ST = p.sem("s_t")
    SV = p.sem("s_v")
    SA = p.sem("s_a")
    SW = p.sem("s_w")
    SC = p.sem("s_c")
    SG = p.sem("s_g")
    SS = p.sem("s_s")
    SP = p.sem("s_p")

    rSLOT = [Region(), Region()]
    rXBF = Region()
    rB_F32 = Region()

    for dst, src in ((cdft, cdft_in), (sdft, sdft_in), (ncdft, ncdft_in),
                     (maskA, maskA_in), (maskB, maskB_in),
                     (gs, gs_in), (us, us_in), (ds, ds_in)):
        p.op("sync", lambda e, d=dst, s=src: e.dma_start(out=d[:], in_=s.ap()),
             inc=SW, inc_n=16)
    consts_w = (SW, SW.count)
    p.op("gpsimd", lambda e: e.memset(ones1[:], 1.0), inc=SC)
    p.op("gpsimd", lambda e: e.memset(onescol[:], 1.0), inc=SC)
    p.op("gpsimd", lambda e: e.memset(onest[:], 1.0), inc=SC)
    memset_w = (SC, SC.count)

    def emit_xbf_load(b, s0, waits):
        src = x_fm.ap()[b, :, s0:s0 + T].rearrange("(c p) t -> p c t", p=128)
        dst = XBF[:].rearrange("p (c t) -> p c t", c=KD)
        p.op("gpsimd", lambda e, s=src, d=dst: e.dma_start(out=d, in_=s),
             waits=waits, inc=SG, inc_n=16)

    def emit_xf_load(b, s0, waits):
        src = x_fm.ap()[b, :, s0:s0 + T].rearrange("(c p) t -> p c t", p=128)
        dst = B[:].rearrange("p (c t) -> p c t", c=KD)
        p.op("scalar", lambda e, s=src, d=dst: e.dma_start(out=d, in_=s),
             waits=waits, inc=SS, inc_n=16)

    emit_xbf_load(0, 0, [])
    emit_xf_load(0, 0, [])

    cast_order = []
    for name, fsrc, bdst, nog in (("wq", wq_f, wq_b, OGD), ("wkv", wkv_f, wkv_b, OGD),
                                  ("wo", wo_f, wo_b, OGD)):
        for g in range(nog):
            cast_order.append((name, g, fsrc, bdst))
    for g in range(OGI):
        cast_order.append(("wg", g, wg_f, wg_b))
        cast_order.append(("wu", g, wu_f, wu_b))
    for g in range(OGDN):
        cast_order.append(("wd", g, wd_f, wd_b))
    cast_at = {}
    for (name, g, fsrc, bdst) in cast_order:
        p.op("gpsimd", lambda e, s=fsrc, d=bdst, gg=g:
             e.dma_start(out=d.ap()[gg], in_=s.ap()[gg]), inc=SC, inc_n=16)
        cast_at[(name, g)] = (SC, SC.count)

    strip_ctr = [0]

    def load_strip(name, bdst, g, width, qd=None):
        sl = strip_ctr[0] % 2
        strip_ctr[0] += 1
        src = bdst.ap()[g] if qd is None else bdst.ap()[g, qd]
        dst = WS[:, sl * SLOTW:sl * SLOTW + width]
        waits = [cast_at[(name, g)]] + rSLOT[sl].require()
        p.op("sync", lambda e, s=src, d=dst: e.dma_start(out=d, in_=s),
             waits=waits, inc=SW, inc_n=16)
        return sl, (SW, SW.count)

    def emit_dbg_taps():
        p.barrier(engines=("scalar",))
        pairs = [("dbg_qbf", QBF[:]), ("dbg_vh", VHBF[:]), ("dbg_xbf", XBF[:]),
                 ("dbg_sqf", SQF[:]), ("dbg_a", A[:]), ("dbg_b", B[:]),
                 ("dbg_hbf", HBF[:]), ("dbg_pref", PREF[:])]
        for k, ap in pairs:
            p.op("scalar", lambda e, d=dbg_out[k], a_=ap:
                 e.dma_start(out=d.ap(), in_=a_), inc=SS, inc_n=16)
        for bb in range(BL):
            for ss0 in range(0, S, T):
                p.op("scalar", lambda e, bb_=bb, s_=ss0:
                     e.dma_start(out=out_fm.ap()[bb_, :, s_:s_ + T].rearrange(
                         "(c p) t -> p c t", p=128),
                         in_=B[:].rearrange("p (c t) -> p c t", c=KD)),
                     inc=SS, inc_n=16)

    def ck(tag, b, ti):
        if _dbg and _dbg.get("stop") == tag and \
                (b * NT + ti) == _dbg.get("ntiles", 8) - 1:
            raise _StopEmit()

    def ln_rows_apply(src_f32, dst_f32, dst_bf, src_bf_buf, src_sq_buf, sq_off,
                      bc_ps, rowb0, rowb1, bcb0, bcb1, row_ws, store_tile):
        for c in range(KD):
            w = [row_ws[c], memset_w] + (rowb0.require() if c == 0 else [])
            p.op("tensor", lambda e, c_=c, st=(c == 0), sp=(c == KD - 1):
                 e.matmul(PSB[0:1, 0:T], onescol[:],
                          src_bf_buf[:, c_ * T:(c_ + 1) * T],
                          start=st, stop=sp, skip_group_check=True),
                 waits=w, inc=ST if c == KD - 1 else None)
        for c in range(KD):
            w = [row_ws[c]] + (rowb1.require() if c == 0 else [])
            p.op("tensor", lambda e, c_=c, st=(c == 0), sp=(c == KD - 1):
                 e.matmul(PSB[0:1, T:2 * T], onescol[:],
                          src_sq_buf[:, sq_off + c_ * T:sq_off + (c_ + 1) * T],
                          start=st, stop=sp, skip_group_check=True),
                 waits=w, inc=ST if c == KD - 1 else None)
        rows_mm = (ST, ST.count)
        p.op("vector", lambda e:
             e.tensor_scalar_mul(ROWSF[0:1, 0:T], PSB[0:1, 0:T], 1.0 / D),
             waits=[rows_mm], inc=SV)
        p.op("vector", lambda e:
             e.tensor_mul(ROWSF[0:1, T:2 * T], ROWSF[0:1, 0:T],
                          ROWSF[0:1, 0:T]), inc=SV)
        p.op("vector", lambda e:
             e.scalar_tensor_tensor(ROWSF[0:1, T:2 * T], PSB[0:1, T:2 * T],
                                    1.0 / D, ROWSF[0:1, T:2 * T],
                                    ALU.mult, ALU.subtract), inc=SV)
        p.op("vector", lambda e:
             e.tensor_scalar_add(ROWSF[0:1, T:2 * T], ROWSF[0:1, T:2 * T], EPS),
             inc=SV)
        p.op("vector", lambda e:
             e.reciprocal(ROWSF[0:1, T:2 * T], ROWSF[0:1, T:2 * T]), inc=SV)
        rowsv = (SV, SV.count)
        rowb0.set([rowsv])
        rowb1.set([rowsv])
        p.op("scalar", lambda e:
             e.activation(ROWSB[0:1, T:2 * T], ROWSF[0:1, T:2 * T], AF.Sqrt),
             waits=[rowsv], inc=SA)
        p.op("scalar", lambda e:
             e.activation(ROWSB[0:1, 0:T], ROWSF[0:1, 0:T], AF.Copy),
             waits=[rowsv], inc=SA)
        rowsb = (SA, SA.count)
        p.op("tensor", lambda e:
             e.matmul(bc_ps[:, 0:T], ones1[:], ROWSB[0:1, 0:T],
                      start=True, stop=True, skip_group_check=True),
             waits=[rowsb, memset_w] + bcb0.require(), inc=ST)
        p.op("tensor", lambda e:
             e.matmul(bc_ps[:, T:2 * T], ones1[:], ROWSB[0:1, T:2 * T],
                      start=True, stop=True, skip_group_check=True),
             waits=[rowsb] + bcb1.require(), inc=ST)
        bc_w = (ST, ST.count)
        apply_ws = []
        for c in range(KD):
            tslot = Ctmp[:, (c % 2) * T:(c % 2 + 1) * T]
            p.op("vector", lambda e, c_=c, ts=tslot:
                 e.tensor_sub(ts, src_f32[:, c_ * T:(c_ + 1) * T], bc_ps[:, 0:T]),
                 waits=[bc_w] if c == 0 else (), inc=SV)
            p.op("vector", lambda e, c_=c, ts=tslot:
                 e.tensor_mul(dst_f32[:, c_ * T:(c_ + 1) * T], ts,
                              bc_ps[:, T:2 * T]), inc=SV)
            app_w = (SV, SV.count)
            if dst_bf is not None:
                p.op("scalar", lambda e, c_=c:
                     e.activation(dst_bf[:, c_ * T:(c_ + 1) * T],
                                  dst_f32[:, c_ * T:(c_ + 1) * T], AF.Copy),
                     waits=[app_w], inc=SA)
                apply_ws.append((SA, SA.count))
            if store_tile is not None:
                bb, ss0 = store_tile
                p.op("scalar", lambda e, c_=c:
                     e.dma_start(out=out_fm.ap()[bb, c_ * 128:(c_ + 1) * 128,
                                                 ss0:ss0 + T],
                                 in_=dst_f32[:, c_ * T:(c_ + 1) * T]),
                     waits=[app_w], inc=SS, inc_n=16)
        lastv = (SV, SV.count)
        bcb0.set([lastv])
        bcb1.set([lastv])
        return apply_ws, lastv

    first_tile = True
    pref_w = {}
    rowb0 = Region()
    rowb1 = Region()
    rD0 = Region()
    rD1 = Region()
    prev_psa = []
    prev_psc = []
    try:
        for b in range(BL):
            for ti in range(NT):
                s0 = ti * T
                if not first_tile:
                    emit_xbf_load(b, s0, rXBF.require())
                    emit_xf_load(b, s0, rB_F32.require())
                first_tile = False
                xbf_w = (SG, SG.count)
                xf_w = (SS, SS.count)

                if ti == 0:
                    p.op("vector", lambda e: e.memset(PREF[:], 0.0),
                         waits=[(SA, SA.count)], inc=SV)
                    pref_w = {}

                def proj_qkv(name, bdst, dst_bf, maskcol, sq_extra,
                             seed0, seed1, mask_reg):
                    pair_r = [Region(), Region()]
                    pair_r[0].set(seed0)
                    pair_r[1].set(seed1)
                    ev_last = None
                    for g in range(OGD):
                        sl, w_w = load_strip(name, bdst, g, KD * 256)
                        ps = PSA if g % 2 == 0 else PSC
                        waits = [w_w, xbf_w] + pair_r[g % 2].require()
                        if g == 0:
                            waits += [consts_w]
                        for k in range(KD):
                            for oc in range(2):
                                lhsT = WS[:, sl * SLOTW + k * 256 + oc * 128:
                                          sl * SLOTW + k * 256 + (oc + 1) * 128]
                                p.op("tensor",
                                     lambda e, o=ps[:, oc * T:(oc + 1) * T],
                                     l=lhsT, r=XBF[:, k * T:(k + 1) * T],
                                     st=(k == 0), sp=(k == KD - 1):
                                     e.matmul(o, l, r, start=st, stop=sp,
                                              skip_group_check=True),
                                     waits=waits if (k == 0 and oc == 0) else (),
                                     inc=ST if (k == KD - 1 and oc == 1) else None)
                        mm_w = (ST, ST.count)
                        rSLOT[sl].set([mm_w])
                        cb = 2 * g
                        p.op("vector", lambda e, cb_=cb, ps_=ps:
                             e.tensor_copy(dst_bf[:, cb_ * T:(cb_ + 2) * T],
                                           ps_[:, 0:1024]),
                             waits=[mm_w], inc=SV)
                        ev_w = (SV, SV.count)
                        sqw = [ev_w] + (sq_extra if g == 0 else [])
                        p.op("scalar", lambda e, cb_=cb:
                             e.activation(HBF[:, SH + cb_ * T:SH + (cb_ + 2) * T],
                                          dst_bf[:, cb_ * T:(cb_ + 2) * T],
                                          AF.Square),
                             waits=sqw, inc=SA)
                        pair_r[g % 2].set([ev_w])
                        ev_last = ev_w
                    sq_w = (SA, SA.count)
                    mwaits = [sq_w, consts_w] + mask_reg.require()
                    for hh in range(16):
                        p.op("tensor", lambda e, h_=hh, st=(hh == 0), sp=(hh == 15):
                             e.matmul(PSB[0:16, maskcol * T:(maskcol + 1) * T],
                                      maskA[:, h_ * 16:(h_ + 1) * 16],
                                      HBF[:, SH + h_ * T:SH + (h_ + 1) * T],
                                      start=st, stop=sp, skip_group_check=True),
                             waits=mwaits if hh == 0 else (),
                             inc=ST if hh == 15 else None)
                    return (ST, ST.count), ev_last, pair_r

                ssq_w, _, qpair = proj_qkv("wq", wq_b, QBF, 0, [],
                                           prev_psa, prev_psc, rowb0)
                ck("Aq", b, ti)
                sskv_w, kv_ev, kvpair = proj_qkv(
                    "wkv", wkv_b, HBF, 1, [ssq_w],
                    qpair[0].require(), qpair[1].require(), rowb1)

                p.op("vector", lambda e:
                     e.reciprocal(SQF[:, 0:2 * T], PSB[0:16, 0:2 * T]),
                     waits=[ssq_w, sskv_w], inc=SV)
                rec_w = (SV, SV.count)
                p.op("scalar", lambda e:
                     e.activation(SQB[:, 0:T], SQF[:, 0:T], AF.Sqrt,
                                  scale=1.0 / float(HD) ** 4),
                     waits=[rec_w], inc=SA)
                p.op("scalar", lambda e:
                     e.activation(SQB[:, T:2 * T], SQF[:, T:2 * T], AF.Sqrt),
                     waits=[rec_w], inc=SA)
                rows_w = (SA, SA.count)
                ck("Arows", b, ti)

                pref_add_t = {}
                mixtr_w = sskv_w
                rB0 = Region(); rB0.set([rec_w])
                rA0 = Region(); rA0.set(qpair[0].require() + kvpair[0].require())
                rA1 = Region(); rA1.set(rA0.require())
                rC0 = Region(); rC0.set(qpair[1].require() + kvpair[1].require())
                rC1 = Region(); rC1.set(rC0.require())
                vh_w_head = []
                ktb_h = {}
                scan_h = {}
                red_h = {}
                kvn_aps = {}
                prodsem = {}
                vh_mul_h = {}
                vh_mm_h = {}
                qcb_h = {}
                qsb_h = {}
                ksb_h = {}

                for s_ in range(17):
                    if s_ < 16:
                        h = s_
                        rot = h % 2
                        p.op("tensor", lambda e, h_=h:
                             e.matmul(PSB[:, 0:T],
                                      maskB[:, h_ * 128:(h_ + 1) * 128],
                                      SQB[:, T:2 * T], start=True, stop=True,
                                      skip_group_check=True),
                             waits=[rows_w] + rB0.require(), inc=ST)
                        bckv_w = (ST, ST.count)
                        kvn = HBF[:, M_KVN + rot * T:M_KVN + (rot + 1) * T]
                        kvn_aps[h] = kvn
                        p.op("vector", lambda e, h_=h, o=kvn:
                             e.tensor_mul(o, HBF[:, h_ * T:(h_ + 1) * T],
                                          PSB[:, 0:T]),
                             waits=[bckv_w, red_h.get(h - 2)] +
                                   ([mixtr_w] if h < 2 else []), inc=SV)
                        rB0.set([(SV, SV.count)])
                        kt = Ctmp[:, rot * T:(rot + 1) * T]
                        p.op("vector", lambda e, o=kt, i=kvn, h_=h:
                             e.tensor_tensor_scan(o, onest[:], i,
                                                  PREF[:, h_:h_ + 1],
                                                  ALU.mult, ALU.add),
                             waits=[memset_w, pref_w.get(h), ktb_h.get(h - 2)],
                             inc=SV)
                        scan_h[h] = (SV, SV.count)
                        p.op("scalar", lambda e, h_=h, i=kt:
                             e.activation(PREF[:, h_:h_ + 1], i[:, T - 1:T],
                                          AF.Copy),
                             waits=[scan_h[h]], inc=SA)
                        pref_w[h] = (SA, SA.count)
                        ktb = HBF[:, M_KTB + rot * T:M_KTB + (rot + 1) * T]
                        p.op("scalar", lambda e, o=ktb, i=kt:
                             e.activation(o, i, AF.Copy),
                             waits=[scan_h[h]], inc=SA)
                        ktb_h[h] = (SA, SA.count)
                    if s_ >= 1:
                        h = s_ - 1
                        ktb = HBF[:, M_KTB + (h % 2) * T:M_KTB + (h % 2 + 1) * T]
                        p.op("tensor", lambda e, h_=h:
                             e.matmul(PSC[:, 0:T], cdft[:],
                                      QBF[:, h_ * T:(h_ + 1) * T],
                                      start=True, stop=True,
                                      skip_group_check=True),
                             waits=rC0.require() + ([mixtr_w] if h == 0 else []),
                             inc=ST)
                        qc_w = (ST, ST.count)
                        p.op("tensor", lambda e, h_=h:
                             e.matmul(PSC[:, T:2 * T], sdft[:],
                                      QBF[:, h_ * T:(h_ + 1) * T],
                                      start=True, stop=True,
                                      skip_group_check=True),
                             waits=rC1.require(), inc=ST)
                        qs_w = (ST, ST.count)
                        p.op("tensor", lambda e, i=ktb:
                             e.matmul(PSD[:, 0:T], cdft[:], i, start=True,
                                      stop=True, skip_group_check=True),
                             waits=[ktb_h[h]] + rD0.require(), inc=ST)
                        kc_w = (ST, ST.count)
                        p.op("tensor", lambda e, i=ktb:
                             e.matmul(PSD[:, T:2 * T], sdft[:], i, start=True,
                                      stop=True, skip_group_check=True),
                             waits=rD1.require(), inc=ST)
                        ks_w = (ST, ST.count)

                        dbase = M_DFT + (h % 2) * 4 * T
                        qcb = HBF[:, dbase:dbase + T]
                        qsb = HBF[:, dbase + T:dbase + 2 * T]
                        kcb = HBF[:, dbase + 2 * T:dbase + 3 * T]
                        ksb = HBF[:, dbase + 3 * T:dbase + 4 * T]
                        pg = prodsem.get(h - 2, {})
                        p.op("scalar", lambda e, o=qcb:
                             e.activation(o, PSC[:, 0:T], AF.Copy),
                             waits=[qc_w, pg.get(1), pg.get(3)], inc=SA)
                        rC0.set([(SA, SA.count)])
                        qcb_h[h] = (SA, SA.count)
                        p.op("vector", lambda e, o=kcb:
                             e.tensor_copy(o, PSD[:, 0:T]),
                             waits=[kc_w], inc=SV)
                        rD0.set([(SV, SV.count)])
                        kcb_h = (SV, SV.count)
                        p.op("scalar", lambda e, o=qsb:
                             e.activation(o, PSC[:, T:2 * T], AF.Copy),
                             waits=[qs_w, pg.get(2), pg.get(4)], inc=SA)
                        rC1.set([(SA, SA.count)])
                        qsb_h[h] = (SA, SA.count)
                        p.op("scalar", lambda e, o=ksb:
                             e.activation(o, PSD[:, T:2 * T], AF.Copy),
                             waits=[ks_w, pg.get(2), pg.get(3)], inc=SA)
                        rD1.set([(SA, SA.count)])
                        ksb_h[h] = (SA, SA.count)

                        pbase = M_P + (h % 2) * 4 * T
                        prods = [HBF[:, pbase + j * T:pbase + (j + 1) * T]
                                 for j in range(4)]
                        slot_free = vh_mm_h.get(h - 2)
                        ps_ = {}
                        p.op("vector", lambda e, o=prods[0], a_=qcb, b_=kcb:
                             e.tensor_mul(o, a_, b_),
                             waits=[qcb_h[h], slot_free], inc=SV)
                        ps_[1] = (SV, SV.count)
                        p.op("gpsimd", lambda e, o=prods[1], a_=qsb, b_=ksb:
                             e.tensor_mul(o, a_, b_),
                             waits=[qsb_h[h], ksb_h[h], slot_free], inc=SP)
                        ps_[2] = (SP, SP.count)
                        p.op("gpsimd", lambda e, o=prods[2], a_=qcb, b_=ksb:
                             e.tensor_mul(o, a_, b_),
                             waits=[qcb_h[h], ksb_h[h], slot_free], inc=SP)
                        ps_[3] = (SP, SP.count)
                        p.op("vector", lambda e, o=prods[3], a_=qsb, b_=kcb:
                             e.tensor_mul(o, a_, b_),
                             waits=[qsb_h[h], slot_free], inc=SV)
                        ps_[4] = (SV, SV.count)
                        prodsem[h] = ps_
                        lhs_list = [cdft, ncdft, sdft, sdft]
                        pws = [ps_[1], ps_[2], ps_[3], ps_[4]]
                        for j in range(4):
                            p.op("tensor", lambda e, l=lhs_list[j], r=prods[j],
                                 st=(j == 0), sp=(j == 3):
                                 e.matmul(PSA[:, 0:T], l[:], r, start=st,
                                          stop=sp, skip_group_check=True),
                                 waits=[pws[j]] +
                                       (rA0.require() if j == 0 else []),
                                 inc=ST if j == 3 else None)
                        vh_w = (ST, ST.count)
                        vh_mm_h[h] = vh_w
                        p.op("tensor", lambda e, h_=h:
                             e.matmul(PSA[:, T:2 * T],
                                      maskB[:, h_ * 128:(h_ + 1) * 128],
                                      SQB[:, 0:T], start=True, stop=True,
                                      skip_group_check=True),
                             waits=[rows_w] + rA1.require(), inc=ST)
                        bcq_w = (ST, ST.count)
                        sqqb = HBF[:, M_SQ + (h % 2) * T:M_SQ + (h % 2 + 1) * T]
                        p.op("scalar", lambda e, o=sqqb:
                             e.activation(o, PSA[:, T:2 * T], AF.Copy),
                             waits=[bcq_w, vh_mul_h.get(h - 2)], inc=SA)
                        sqqb_w = (SA, SA.count)
                        rA1.set([sqqb_w])
                        p.op("vector", lambda e, h_=h, s_a=sqqb:
                             e.tensor_mul(VHBF[:, h_ * T:(h_ + 1) * T],
                                          PSA[:, 0:T], s_a),
                             waits=[vh_w, sqqb_w], inc=SV)
                        rA0.set([(SV, SV.count)])
                        vh_mul_h[h] = (SV, SV.count)
                        vh_w_head.append((SV, SV.count))
                        ck(f"Ah{h}", b, ti)
                vh_done = (SV, SV.count)
                ck("A", b, ti)

                pair_r = [Region(), Region()]
                pair_r[0].set(rA0.require() + rA1.require())
                pair_r[1].set(rC0.require() + rC1.require())
                x2a_ws = []
                for g in range(OGD):
                    sl, w_w = load_strip("wo", wo_b, g, KD * 256)
                    ps = PSA if g % 2 == 0 else PSC
                    waits = [w_w] + pair_r[g % 2].require()
                    for k in range(KD):
                        if g < 2:
                            kw = (waits + [vh_w_head[k]]) if k == 0 else \
                                [vh_w_head[k]]
                        else:
                            kw = waits if k == 0 else ()
                        for oc in range(2):
                            lhsT = WS[:, sl * SLOTW + k * 256 + oc * 128:
                                      sl * SLOTW + k * 256 + (oc + 1) * 128]
                            p.op("tensor",
                                 lambda e, o=ps[:, oc * T:(oc + 1) * T],
                                 l=lhsT, r=VHBF[:, k * T:(k + 1) * T],
                                 st=(k == 0), sp=(k == KD - 1):
                                 e.matmul(o, l, r, start=st, stop=sp,
                                          skip_group_check=True),
                                 waits=kw if oc == 0 else (),
                                 inc=ST if (k == KD - 1 and oc == 1) else None)
                    mm_w = (ST, ST.count)
                    rSLOT[sl].set([mm_w])
                    cb = 2 * g
                    p.op("vector", lambda e, cb_=cb, ps_=ps:
                         e.tensor_add(A[:, cb_ * T:(cb_ + 2) * T], ps_[:, 0:1024],
                                      B[:, cb_ * T:(cb_ + 2) * T]),
                         waits=[mm_w, xf_w], inc=SV)
                    x2a_w = (SV, SV.count)
                    pair_r[g % 2].set([x2a_w])
                    p.op("scalar", lambda e, cb_=cb:
                         e.activation(HBF[:, cb_ * T:(cb_ + 2) * T],
                                      A[:, cb_ * T:(cb_ + 2) * T], AF.Copy),
                         waits=[x2a_w] + ([vh_done] if g == 0 else []), inc=SA)
                    p.op("scalar", lambda e, cb_=cb:
                         e.activation(HBF[:, SH + cb_ * T:SH + (cb_ + 2) * T],
                                      A[:, cb_ * T:(cb_ + 2) * T], AF.Square),
                         inc=SA)
                    w2 = (SA, SA.count)
                    x2a_ws.extend([w2, w2])

                rowb0.set(rB0.require())
                rowb1.set([rec_w])
                x1_ws, ln1_v = ln_rows_apply(
                    A, B, XBF, HBF, HBF, SH, PSD, rowb0, rowb1, rD0, rD1,
                    x2a_ws, None)
                rB_F32.set([])
                ck("B", b, ti)

                hprev = {}
                h_ws = []
                gpair = [Region(), Region()]
                gpair[0].set(pair_r[0].require())
                gpair[1].set(pair_r[1].require())
                upair = [Region(), Region()]
                upair[0].set(rowb0.require() + rowb1.require())
                upair[1].set([ln1_v])
                for g in range(OGI):
                    slg, wg_w = load_strip("wg", wg_b, g, KD * 256)
                    psg = PSA if g % 2 == 0 else PSC
                    waits = [wg_w] + gpair[g % 2].require()
                    for k in range(KD):
                        if g < 2:
                            kw = (waits + [x1_ws[k]]) if k == 0 else [x1_ws[k]]
                        else:
                            kw = waits if k == 0 else ()
                        for oc in range(2):
                            lhsT = WS[:, slg * SLOTW + k * 256 + oc * 128:
                                      slg * SLOTW + k * 256 + (oc + 1) * 128]
                            p.op("tensor",
                                 lambda e, o=psg[:, oc * T:(oc + 1) * T],
                                 l=lhsT, r=XBF[:, k * T:(k + 1) * T],
                                 st=(k == 0), sp=(k == KD - 1):
                                 e.matmul(o, l, r, start=st, stop=sp,
                                          skip_group_check=True),
                                 waits=kw if oc == 0 else (),
                                 inc=ST if (k == KD - 1 and oc == 1) else None)
                    g_w = (ST, ST.count)
                    rSLOT[slg].set([g_w])
                    slu, wu_w = load_strip("wu", wu_b, g, KD * 256)
                    psu = PSB if g % 2 == 0 else PSD
                    waitsu = [wu_w] + upair[g % 2].require()
                    for k in range(KD):
                        if g < 2:
                            kw = (waitsu + [x1_ws[k]]) if k == 0 else [x1_ws[k]]
                        else:
                            kw = waitsu if k == 0 else ()
                        for oc in range(2):
                            lhsT = WS[:, slu * SLOTW + k * 256 + oc * 128:
                                      slu * SLOTW + k * 256 + (oc + 1) * 128]
                            p.op("tensor",
                                 lambda e, o=psu[:, oc * T:(oc + 1) * T],
                                 l=lhsT, r=XBF[:, k * T:(k + 1) * T],
                                 st=(k == 0), sp=(k == KD - 1):
                                 e.matmul(o, l, r, start=st, stop=sp,
                                          skip_group_check=True),
                                 waits=kw if oc == 0 else (),
                                 inc=ST if (k == KD - 1 and oc == 1) else None)
                    u_w = (ST, ST.count)
                    rSLOT[slu].set([u_w])
                    for oc in range(2):
                        c = 2 * g + oc
                        rot = c % 2
                        sg_t = QBF[:, rot * T:(rot + 1) * T]
                        u_t = QBF[:, (2 + rot) * T:(3 + rot) * T]
                        hw = [hprev[rot]] if c >= 2 else []
                        p.op("scalar", lambda e, o=sg_t, oc_=oc, c_=c, ps_=psg:
                             e.activation(o, ps_[:, oc_ * T:(oc_ + 1) * T],
                                          AF.Silu, scale=gs[:, c_:c_ + 1]),
                             waits=[g_w] + hw, inc=SA)
                        p.op("scalar", lambda e, o=u_t, oc_=oc, c_=c, ps_=psu:
                             e.mul(o, ps_[:, oc_ * T:(oc_ + 1) * T],
                                   us[:, c_:c_ + 1]),
                             waits=[u_w], inc=SA)
                        su_w = (SA, SA.count)
                        p.op("vector", lambda e, c_=c, a_=sg_t, b_=u_t:
                             e.tensor_mul(HBF[:, c_ * T:(c_ + 1) * T], a_, b_),
                             waits=[su_w], inc=SV)
                        hprev[rot] = (SV, SV.count)
                        h_ws.append((SV, SV.count))
                    gpair[g % 2].set([(SA, SA.count)])
                    upair[g % 2].set([(SA, SA.count)])
                rXBF.set([(ST, ST.count)])
                ck("C", b, ti)

                bank_regs = [Region(), Region(), Region(), Region()]
                bank_regs[0].set(gpair[0].require())
                bank_regs[1].set(gpair[0].require())
                bank_regs[2].set(gpair[1].require())
                bank_regs[3].set(gpair[1].require())
                bank_aps = [(PSA, 0), (PSA, 1), (PSC, 0), (PSC, 1)]
                x2b_ws = []
                rQ = [Region(), Region(), Region(), Region()]
                rQ[0].set(rSLOT[0].require())
                rQ[1].set(rSLOT[0].require())
                rQ[2].set(rSLOT[1].require())
                rQ[3].set(rSLOT[1].require())
                qbase = [0, 2048, SLOTW, SLOTW + 2048]
                qctr = [0]
                qmap = {}

                def load_q(g_, qd):
                    ss = qctr[0] % 4
                    qctr[0] += 1
                    dst = WS[:, qbase[ss]:qbase[ss] + KH * 128]
                    waits = [cast_at[("wd", g_)]] + rQ[ss].require()
                    p.op("sync", lambda e, s=wd_b.ap()[g_, qd], d=dst:
                         e.dma_start(out=d, in_=s),
                         waits=waits, inc=SW, inc_n=16)
                    qmap[(g_, qd)] = (ss, (SW, SW.count))

                for qd in range(4):
                    load_q(0, qd)
                for g in range(OGDN):
                    ps, half = bank_aps[g % 4]
                    waits = [qmap[(g, 0)][1]] + bank_regs[g % 4].require()
                    for k in range(KI):
                        qd = k // KH
                        lastq = (k % KH == KH - 1)
                        ss, q_w = qmap[(g, qd)]
                        lhsT = WS[:, qbase[ss] + (k % KH) * 128:
                                  qbase[ss] + (k % KH + 1) * 128]
                        kw = list(waits) if k == 0 else \
                            ([q_w] if k % KH == 0 else [])
                        if g < 4:
                            kw = kw + [h_ws[k]]
                        p.op("tensor",
                             lambda e, o=ps[:, half * T:(half + 1) * T],
                             l=lhsT, k_=k, st=(k == 0), sp=(k == KI - 1):
                             e.matmul(o, l, HBF[:, k_ * T:(k_ + 1) * T],
                                      start=st, stop=sp, skip_group_check=True),
                             waits=kw, inc=ST if lastq else None)
                        if lastq:
                            rQ[ss].set([(ST, ST.count)])
                            nxt = g * 4 + qd + 4
                            if nxt < OGDN * 4:
                                load_q(nxt // 4, nxt % 4)
                    mm_w = (ST, ST.count)
                    c = g
                    p.op("vector", lambda e, c_=c, o=A[:, c * T:(c + 1) * T],
                         i=ps[:, half * T:(half + 1) * T]:
                         e.scalar_tensor_tensor(o, i, ds[:, c_:c_ + 1],
                                                B[:, c_ * T:(c_ + 1) * T],
                                                ALU.mult, ALU.add),
                         waits=[mm_w], inc=SV)
                    x2b_w = (SV, SV.count)
                    bank_regs[g % 4].set([x2b_w])
                    p.op("scalar", lambda e, c_=c:
                         e.activation(VHBF[:, c_ * T:(c_ + 1) * T],
                                      A[:, c_ * T:(c_ + 1) * T], AF.Copy),
                         waits=[x2b_w], inc=SA)
                    p.op("scalar", lambda e, c_=c:
                         e.activation(QBF[:, c_ * T:(c_ + 1) * T],
                                      A[:, c_ * T:(c_ + 1) * T], AF.Square),
                         waits=[hprev[0], hprev[1]] if g == 0 else (), inc=SA)
                    w2 = (SA, SA.count)
                    x2b_ws.append(w2)

                rowb0.set(upair[0].require())
                rowb1.set(upair[0].require())
                rD0.set(upair[1].require())
                rD1.set(upair[1].require())
                _, ln2_v = ln_rows_apply(
                    A, B, None, VHBF, QBF, 0, PSD, rowb0, rowb1, rD0, rD1,
                    x2b_ws, (b, s0))
                store_done = (SS, SS.count)
                rB_F32.set([store_done])
                rSLOT[0].set([mm_w])
                rSLOT[1].set([mm_w])
                prev_psa = bank_regs[0].require() + bank_regs[1].require()
                prev_psc = bank_regs[2].require() + bank_regs[3].require()
                ck("D", b, ti)
                p.barrier(only=("s_t", "s_v", "s_a", "s_p"))
    except _StopEmit:
        pass
    if _dbg:
        emit_dbg_taps()

    def fin(eng):
        eng.wait_ge(SS.h, SS.count)
    p.streams["scalar"].append(fin)
    p.finalize()
    return nc


def _prep_host(inputs):
    x = np.ascontiguousarray(np.asarray(inputs["L0_attn_x"], dtype=np.float32))
    wq = np.asarray(inputs["L0_attn_W_q"], dtype=np.float32)
    wkv = np.asarray(inputs["L0_attn_W_kv"], dtype=np.float32)
    wo = np.asarray(inputs["L0_attn_W_o"], dtype=np.float32)
    gw = np.asarray(inputs["L0_ffn_gate_w"], dtype=np.float32)
    uw = np.asarray(inputs["L0_ffn_up_w"], dtype=np.float32)
    dw = np.asarray(inputs["L0_ffn_down_w"], dtype=np.float32)
    gsv = np.asarray(inputs["L0_ffn_gate_s"], dtype=np.float32)
    usv = np.asarray(inputs["L0_ffn_up_s"], dtype=np.float32)
    dsv = np.asarray(inputs["L0_ffn_down_s"], dtype=np.float32)

    def strip_d(w, nog):
        return np.ascontiguousarray(
            w.T.reshape(KD, 128, nog, 256).transpose(2, 1, 0, 3).reshape(
                nog, 128, KD * 256))

    wd_s = np.ascontiguousarray(
        dw.T.reshape(4, KH, 128, OGDN, 128).transpose(3, 0, 2, 1, 4).reshape(
            OGDN, 4, 128, KH * 128))

    m = np.arange(128)
    ang = 2.0 * np.pi * np.outer(m, m) / 128.0
    cdft = np.cos(ang).astype(np.float32).astype(ml_dtypes.bfloat16)
    sdft = np.sin(ang).astype(np.float32).astype(ml_dtypes.bfloat16)
    ncdft = (-np.cos(ang)).astype(np.float32).astype(ml_dtypes.bfloat16)
    maskA = np.zeros((128, 16, 16), dtype=ml_dtypes.bfloat16)
    for h in range(16):
        maskA[:, h, h] = 1
    maskB = np.zeros((16, 16, 128), dtype=ml_dtypes.bfloat16)
    for h in range(16):
        maskB[h, h, :] = 1

    shared = dict(
        wq_f=strip_d(wq, OGD), wkv_f=strip_d(wkv, OGD), wo_f=strip_d(wo, OGD),
        wg_f=strip_d(gw, OGI), wu_f=strip_d(uw, OGI), wd_f=wd_s,
        gs_in=np.ascontiguousarray(gsv.reshape(KI, 128).T),
        us_in=np.ascontiguousarray(usv.reshape(KI, 128).T),
        ds_in=np.ascontiguousarray(dsv.reshape(KD, 128).T),
        cdft_in=cdft, sdft_in=sdft, ncdft_in=ncdft,
        maskA_in=np.ascontiguousarray(maskA.reshape(128, 256)),
        maskB_in=np.ascontiguousarray(maskB.reshape(16, 2048)),
    )
    in_maps = []
    for i in range(NCORES):
        xc = np.ascontiguousarray(x[BL * i:BL * (i + 1)].transpose(0, 2, 1))
        im = dict(shared)
        im["x_fm"] = xc
        in_maps.append(im)
    return in_maps


def kernel(**inputs):
    in_maps = _prep_host(inputs)
    nc = build_nc()
    res = run_bass_kernel_spmd(nc, in_maps, core_ids=list(range(NCORES)))
    out = np.empty((16, S, D), dtype=np.float32)
    for i in range(NCORES):
        o = np.asarray(res.results[i]["out_fm"])
        out[BL * i:BL * (i + 1)] = o.transpose(0, 2, 1)
    return out


# version 16
# speedup vs baseline: 1.1406x; 1.0160x over previous
import numpy as np
import ml_dtypes
import concourse.bass as bass
import concourse.mybir as mybir
from concourse.bass_utils import run_bass_kernel_spmd

F32 = mybir.dt.float32
BF16 = mybir.dt.bfloat16
AF = mybir.ActivationFunctionType
ALU = mybir.AluOpType
AX = mybir.AxisListType

D = 2048
S = 2048
H = 16
HD = 128
INTER = 5632
T = 512
NT = S // T
BL = 2
KD = D // 128
KI = INTER // 128
KH = KI // 4
OGD = 8
OGI = 22
OGDN = 16
NCORES = 8
EPS = 1e-5
SLOTW = 4096


class Sem:
    def __init__(self, nc, name):
        self.h = nc.alloc_semaphore(name)
        self.count = 0
        self.name = name


class Region:
    def __init__(self):
        self.req = []

    def require(self):
        return list(self.req)

    def set(self, reqs):
        self.req = [r for r in reqs if r is not None]


class Plan:
    ENGINES = ("tensor", "vector", "scalar", "sync", "gpsimd")

    def __init__(self, nc):
        self.nc = nc
        self.streams = {e: [] for e in self.ENGINES}
        self.sems = []

    def sem(self, name):
        s = Sem(self.nc, name)
        self.sems.append(s)
        return s

    def op(self, engine, build, waits=(), inc=None, inc_n=1):
        wmap = {}
        for w in waits:
            if w is None:
                continue
            s, v = w
            assert v <= s.count, f"wait on {s.name} for {v} > emitted {s.count}"
            if v > 0 and (s.h not in wmap or wmap[s.h] < v):
                wmap[s.h] = v
        if inc is not None:
            inc.count += inc_n
            inc_h = inc.h
        else:
            inc_h = None
        wlist = list(wmap.items())

        def fn(eng, _wl=wlist, _build=build, _ih=inc_h, _n=inc_n):
            for h, v in _wl:
                eng.wait_ge(h, v)
            inst = _build(eng)
            if _ih is not None:
                inst.then_inc(_ih, _n)
        self.streams[engine].append(fn)

    def barrier(self, engines=("tensor", "vector", "scalar", "gpsimd"),
                only=None):
        snap = [(s.h, s.count) for s in self.sems
                if s.count > 0 and (only is None or s.name in only)]
        for e in engines:
            def fn(eng, _snap=list(snap)):
                for h, v in _snap:
                    eng.wait_ge(h, v)
            self.streams[e].append(fn)

    def finalize(self):
        nc = self.nc
        with nc.Block() as block:
            block.tensor(lambda t: [f(t) for f in self.streams["tensor"]])
            block.vector(lambda v: [f(v) for f in self.streams["vector"]])
            block.scalar(lambda s: [f(s) for f in self.streams["scalar"]])
            block.sync(lambda s: [f(s) for f in self.streams["sync"]])
            block.gpsimd(lambda g: [f(g) for f in self.streams["gpsimd"]])


class _StopEmit(Exception):
    pass


def build_nc(_dbg=None):
    nc = bass.Bass("TRN2", target_bir_lowering=False)

    x_fm = nc.dram_tensor("x_fm", [BL, D, S], F32, kind="ExternalInput")
    wq_f = nc.dram_tensor("wq_f", [OGD, 128, KD * 256], F32, kind="ExternalInput")
    wkv_f = nc.dram_tensor("wkv_f", [OGD, 128, KD * 256], F32, kind="ExternalInput")
    wo_f = nc.dram_tensor("wo_f", [OGD, 128, KD * 256], F32, kind="ExternalInput")
    wg_f = nc.dram_tensor("wg_f", [OGI, 128, KD * 256], F32, kind="ExternalInput")
    wu_f = nc.dram_tensor("wu_f", [OGI, 128, KD * 256], F32, kind="ExternalInput")
    wd_f = nc.dram_tensor("wd_f", [OGDN, 4, 128, KH * 128], F32, kind="ExternalInput")
    gs_in = nc.dram_tensor("gs_in", [128, KI], F32, kind="ExternalInput")
    us_in = nc.dram_tensor("us_in", [128, KI], F32, kind="ExternalInput")
    ds_in = nc.dram_tensor("ds_in", [128, KD], F32, kind="ExternalInput")
    cdft_in = nc.dram_tensor("cdft_in", [128, 128], BF16, kind="ExternalInput")
    sdft_in = nc.dram_tensor("sdft_in", [128, 128], BF16, kind="ExternalInput")
    ncdft_in = nc.dram_tensor("ncdft_in", [128, 128], BF16, kind="ExternalInput")
    maskA_in = nc.dram_tensor("maskA_in", [128, 256], BF16, kind="ExternalInput")
    maskB_in = nc.dram_tensor("maskB_in", [16, 2048], BF16, kind="ExternalInput")
    out_fm = nc.dram_tensor("out_fm", [BL, D, S], F32, kind="ExternalOutput")
    dbg_out = None
    if _dbg:
        dbg_t = {
            "dbg_qbf": ([128, 8192], BF16), "dbg_vh": ([128, 8192], BF16),
            "dbg_xbf": ([128, 8192], BF16), "dbg_sqf": ([16, 1024], F32),
            "dbg_a": ([128, 8192], F32), "dbg_b": ([128, 8192], F32),
            "dbg_hbf": ([128, 22528], BF16), "dbg_pref": ([128, 16], F32),
        }
        dbg_out = {k: nc.dram_tensor(k, sh, dt, kind="ExternalOutput")
                   for k, (sh, dt) in dbg_t.items()}

    wq_b = nc.dram_tensor("wq_b", [OGD, 128, KD * 256], BF16, kind="Internal")
    wkv_b = nc.dram_tensor("wkv_b", [OGD, 128, KD * 256], BF16, kind="Internal")
    wo_b = nc.dram_tensor("wo_b", [OGD, 128, KD * 256], BF16, kind="Internal")
    wg_b = nc.dram_tensor("wg_b", [OGI, 128, KD * 256], BF16, kind="Internal")
    wu_b = nc.dram_tensor("wu_b", [OGI, 128, KD * 256], BF16, kind="Internal")
    wd_b = nc.dram_tensor("wd_b", [OGDN, 4, 128, KH * 128], BF16, kind="Internal")

    A = nc.alloc_sbuf_tensor("buf_a", [128, KD * T], F32)
    B = nc.alloc_sbuf_tensor("buf_b", [128, KD * T], F32)
    Ctmp = nc.alloc_sbuf_tensor("buf_c", [128, 2 * T], F32)
    SQF = nc.alloc_sbuf_tensor("sq_f", [16, 2 * T], F32)
    SQB = nc.alloc_sbuf_tensor("sq_b", [16, 2 * T], BF16)
    ROWSF = SQF
    ROWSB = SQB
    PREF = nc.alloc_sbuf_tensor("prefix", [128, 16], F32)
    PREFT = nc.alloc_sbuf_tensor("preft", [128, 4], F32)
    XBF = nc.alloc_sbuf_tensor("x_bf", [128, KD * T], BF16)
    QBF = nc.alloc_sbuf_tensor("q_bf", [128, KD * T], BF16)
    VHBF = nc.alloc_sbuf_tensor("vh_bf", [128, KD * T], BF16)
    HBF = nc.alloc_sbuf_tensor("h_bf", [128, KI * T], BF16)
    WS = nc.alloc_sbuf_tensor("wslot", [128, 2 * SLOTW], BF16)
    cdft = nc.alloc_sbuf_tensor("cdft", [128, 128], BF16)
    sdft = nc.alloc_sbuf_tensor("sdft", [128, 128], BF16)
    ncdft = nc.alloc_sbuf_tensor("ncdft", [128, 128], BF16)
    maskA = nc.alloc_sbuf_tensor("maskA", [128, 256], BF16)
    maskB = nc.alloc_sbuf_tensor("maskB", [16, 2048], BF16)
    gs = nc.alloc_sbuf_tensor("gs", [128, KI], F32)
    us = nc.alloc_sbuf_tensor("us", [128, KI], F32)
    ds = nc.alloc_sbuf_tensor("ds", [128, KD], F32)
    ones1 = nc.alloc_sbuf_tensor("ones1", [1, 128], BF16)
    onescol = nc.alloc_sbuf_tensor("onescol", [128, 1], BF16)
    onest = nc.alloc_sbuf_tensor("onest", [128, T], BF16)

    SH = KD * T
    M_QC = SH + 0
    M_QS = SH + T
    M_KC = SH + 2 * T
    M_KS = SH + 3 * T
    M_P = SH + 4 * T
    M_KVN = SH + 12 * T
    M_KTB = SH + 14 * T
    M_SQ = SH + 16 * T
    M_DFT = SH + 18 * T

    PSA = nc.alloc_psum_tensor("psa", [128, 1024], F32)
    PSB = nc.alloc_psum_tensor("psb", [128, 1024], F32)
    PSC = nc.alloc_psum_tensor("psc", [128, 1024], F32)
    PSD = nc.alloc_psum_tensor("psd", [128, 1024], F32)

    p = Plan(nc)
    ST = p.sem("s_t")
    SV = p.sem("s_v")
    SA = p.sem("s_a")
    SW = p.sem("s_w")
    SC = p.sem("s_c")
    SG = p.sem("s_g")
    SS = p.sem("s_s")
    SP = p.sem("s_p")

    rSLOT = [Region(), Region()]
    rXBF = Region()
    rB_F32 = Region()

    for dst, src in ((cdft, cdft_in), (sdft, sdft_in), (ncdft, ncdft_in),
                     (maskA, maskA_in), (maskB, maskB_in),
                     (gs, gs_in), (us, us_in), (ds, ds_in)):
        p.op("sync", lambda e, d=dst, s=src: e.dma_start(out=d[:], in_=s.ap()),
             inc=SW, inc_n=16)
    consts_w = (SW, SW.count)
    p.op("gpsimd", lambda e: e.memset(ones1[:], 1.0), inc=SC)
    p.op("gpsimd", lambda e: e.memset(onescol[:], 1.0), inc=SC)
    p.op("gpsimd", lambda e: e.memset(onest[:], 1.0), inc=SC)
    memset_w = (SC, SC.count)

    def emit_xbf_load(b, s0, waits):
        src = x_fm.ap()[b, :, s0:s0 + T].rearrange("(c p) t -> p c t", p=128)
        dst = XBF[:].rearrange("p (c t) -> p c t", c=KD)
        p.op("gpsimd", lambda e, s=src, d=dst: e.dma_start(out=d, in_=s),
             waits=waits, inc=SG, inc_n=16)

    def emit_xf_load(b, s0, waits):
        src = x_fm.ap()[b, :, s0:s0 + T].rearrange("(c p) t -> p c t", p=128)
        dst = B[:].rearrange("p (c t) -> p c t", c=KD)
        p.op("scalar", lambda e, s=src, d=dst: e.dma_start(out=d, in_=s),
             waits=waits, inc=SS, inc_n=16)

    emit_xbf_load(0, 0, [])
    emit_xf_load(0, 0, [])

    cast_order = []
    for name, fsrc, bdst, nog in (("wq", wq_f, wq_b, OGD), ("wkv", wkv_f, wkv_b, OGD),
                                  ("wo", wo_f, wo_b, OGD)):
        for g in range(nog):
            cast_order.append((name, g, fsrc, bdst))
    for g in range(OGI):
        cast_order.append(("wg", g, wg_f, wg_b))
        cast_order.append(("wu", g, wu_f, wu_b))
    for g in range(OGDN):
        cast_order.append(("wd", g, wd_f, wd_b))
    cast_at = {}
    for (name, g, fsrc, bdst) in cast_order:
        p.op("gpsimd", lambda e, s=fsrc, d=bdst, gg=g:
             e.dma_start(out=d.ap()[gg], in_=s.ap()[gg]), inc=SC, inc_n=16)
        cast_at[(name, g)] = (SC, SC.count)

    strip_ctr = [0]

    def load_strip(name, bdst, g, width, qd=None):
        sl = strip_ctr[0] % 2
        strip_ctr[0] += 1
        src = bdst.ap()[g] if qd is None else bdst.ap()[g, qd]
        dst = WS[:, sl * SLOTW:sl * SLOTW + width]
        waits = [cast_at[(name, g)]] + rSLOT[sl].require()
        p.op("sync", lambda e, s=src, d=dst: e.dma_start(out=d, in_=s),
             waits=waits, inc=SW, inc_n=16)
        return sl, (SW, SW.count)

    def emit_dbg_taps():
        p.barrier(engines=("scalar",))
        pairs = [("dbg_qbf", QBF[:]), ("dbg_vh", VHBF[:]), ("dbg_xbf", XBF[:]),
                 ("dbg_sqf", SQF[:]), ("dbg_a", A[:]), ("dbg_b", B[:]),
                 ("dbg_hbf", HBF[:]), ("dbg_pref", PREF[:])]
        for k, ap in pairs:
            p.op("scalar", lambda e, d=dbg_out[k], a_=ap:
                 e.dma_start(out=d.ap(), in_=a_), inc=SS, inc_n=16)
        for bb in range(BL):
            for ss0 in range(0, S, T):
                p.op("scalar", lambda e, bb_=bb, s_=ss0:
                     e.dma_start(out=out_fm.ap()[bb_, :, s_:s_ + T].rearrange(
                         "(c p) t -> p c t", p=128),
                         in_=B[:].rearrange("p (c t) -> p c t", c=KD)),
                     inc=SS, inc_n=16)

    def ck(tag, b, ti):
        if _dbg and _dbg.get("stop") == tag and \
                (b * NT + ti) == _dbg.get("ntiles", 8) - 1:
            raise _StopEmit()

    def ln_rows_apply(src_f32, dst_f32, dst_bf, src_bf_buf, src_sq_buf, sq_off,
                      bc_ps, rowb0, rowb1, bcb0, bcb1, row_ws, store_tile):
        for c in range(KD):
            w = [row_ws[c], memset_w] + (rowb0.require() if c == 0 else [])
            p.op("tensor", lambda e, c_=c, st=(c == 0), sp=(c == KD - 1):
                 e.matmul(PSB[0:1, 0:T], onescol[:],
                          src_bf_buf[:, c_ * T:(c_ + 1) * T],
                          start=st, stop=sp, skip_group_check=True),
                 waits=w, inc=ST if c == KD - 1 else None)
        for c in range(KD):
            w = [row_ws[c]] + (rowb1.require() if c == 0 else [])
            p.op("tensor", lambda e, c_=c, st=(c == 0), sp=(c == KD - 1):
                 e.matmul(PSB[0:1, T:2 * T], onescol[:],
                          src_sq_buf[:, sq_off + c_ * T:sq_off + (c_ + 1) * T],
                          start=st, stop=sp, skip_group_check=True),
                 waits=w, inc=ST if c == KD - 1 else None)
        rows_mm = (ST, ST.count)
        p.op("vector", lambda e:
             e.tensor_scalar_mul(ROWSF[0:1, 0:T], PSB[0:1, 0:T], 1.0 / D),
             waits=[rows_mm], inc=SV)
        p.op("vector", lambda e:
             e.tensor_mul(ROWSF[0:1, T:2 * T], ROWSF[0:1, 0:T],
                          ROWSF[0:1, 0:T]), inc=SV)
        p.op("vector", lambda e:
             e.scalar_tensor_tensor(ROWSF[0:1, T:2 * T], PSB[0:1, T:2 * T],
                                    1.0 / D, ROWSF[0:1, T:2 * T],
                                    ALU.mult, ALU.subtract), inc=SV)
        p.op("vector", lambda e:
             e.tensor_scalar_add(ROWSF[0:1, T:2 * T], ROWSF[0:1, T:2 * T], EPS),
             inc=SV)
        p.op("vector", lambda e:
             e.reciprocal(ROWSF[0:1, T:2 * T], ROWSF[0:1, T:2 * T]), inc=SV)
        rowsv = (SV, SV.count)
        rowb0.set([rowsv])
        rowb1.set([rowsv])
        p.op("scalar", lambda e:
             e.activation(ROWSB[0:1, T:2 * T], ROWSF[0:1, T:2 * T], AF.Sqrt),
             waits=[rowsv], inc=SA)
        p.op("scalar", lambda e:
             e.activation(ROWSB[0:1, 0:T], ROWSF[0:1, 0:T], AF.Copy),
             waits=[rowsv], inc=SA)
        rowsb = (SA, SA.count)
        p.op("tensor", lambda e:
             e.matmul(bc_ps[:, 0:T], ones1[:], ROWSB[0:1, 0:T],
                      start=True, stop=True, skip_group_check=True),
             waits=[rowsb, memset_w] + bcb0.require(), inc=ST)
        p.op("tensor", lambda e:
             e.matmul(bc_ps[:, T:2 * T], ones1[:], ROWSB[0:1, T:2 * T],
                      start=True, stop=True, skip_group_check=True),
             waits=[rowsb] + bcb1.require(), inc=ST)
        bc_w = (ST, ST.count)
        apply_ws = []
        for c in range(KD):
            tslot = Ctmp[:, (c % 2) * T:(c % 2 + 1) * T]
            p.op("vector", lambda e, c_=c, ts=tslot:
                 e.tensor_sub(ts, src_f32[:, c_ * T:(c_ + 1) * T], bc_ps[:, 0:T]),
                 waits=[bc_w] if c == 0 else (), inc=SV)
            p.op("vector", lambda e, c_=c, ts=tslot:
                 e.tensor_mul(dst_f32[:, c_ * T:(c_ + 1) * T], ts,
                              bc_ps[:, T:2 * T]), inc=SV)
            app_w = (SV, SV.count)
            if dst_bf is not None:
                p.op("scalar", lambda e, c_=c:
                     e.activation(dst_bf[:, c_ * T:(c_ + 1) * T],
                                  dst_f32[:, c_ * T:(c_ + 1) * T], AF.Copy),
                     waits=[app_w], inc=SA)
                apply_ws.append((SA, SA.count))
            if store_tile is not None:
                bb, ss0 = store_tile
                p.op("scalar", lambda e, c_=c:
                     e.dma_start(out=out_fm.ap()[bb, c_ * 128:(c_ + 1) * 128,
                                                 ss0:ss0 + T],
                                 in_=dst_f32[:, c_ * T:(c_ + 1) * T]),
                     waits=[app_w], inc=SS, inc_n=16)
        lastv = (SV, SV.count)
        bcb0.set([lastv])
        bcb1.set([lastv])
        return apply_ws, lastv

    first_tile = True
    pref_w = {}
    rowb0 = Region()
    rowb1 = Region()
    rD0 = Region()
    rD1 = Region()
    prev_psa = []
    prev_psc = []
    try:
        for b in range(BL):
            for ti in range(NT):
                s0 = ti * T
                if not first_tile:
                    emit_xbf_load(b, s0, rXBF.require())
                    emit_xf_load(b, s0, rB_F32.require())
                first_tile = False
                xbf_w = (SG, SG.count)
                xf_w = (SS, SS.count)

                if ti == 0:
                    p.op("vector", lambda e: e.memset(PREF[:], 0.0),
                         waits=[(SA, SA.count)], inc=SV)
                    pref_w = {}

                def proj_qkv(name, bdst, dst_bf, maskcol, sq_extra,
                             seed0, seed1, mask_reg):
                    pair_r = [Region(), Region()]
                    pair_r[0].set(seed0)
                    pair_r[1].set(seed1)
                    ev_last = None
                    for g in range(OGD):
                        sl, w_w = load_strip(name, bdst, g, KD * 256)
                        ps = PSA if g % 2 == 0 else PSC
                        waits = [w_w, xbf_w] + pair_r[g % 2].require()
                        if g == 0:
                            waits += [consts_w]
                        for k in range(KD):
                            for oc in range(2):
                                lhsT = WS[:, sl * SLOTW + k * 256 + oc * 128:
                                          sl * SLOTW + k * 256 + (oc + 1) * 128]
                                p.op("tensor",
                                     lambda e, o=ps[:, oc * T:(oc + 1) * T],
                                     l=lhsT, r=XBF[:, k * T:(k + 1) * T],
                                     st=(k == 0), sp=(k == KD - 1):
                                     e.matmul(o, l, r, start=st, stop=sp,
                                              skip_group_check=True),
                                     waits=waits if (k == 0 and oc == 0) else (),
                                     inc=ST if (k == KD - 1 and oc == 1) else None)
                        mm_w = (ST, ST.count)
                        rSLOT[sl].set([mm_w])
                        cb = 2 * g
                        p.op("vector", lambda e, cb_=cb, ps_=ps:
                             e.tensor_copy(dst_bf[:, cb_ * T:(cb_ + 2) * T],
                                           ps_[:, 0:1024]),
                             waits=[mm_w], inc=SV)
                        ev_w = (SV, SV.count)
                        sqw = [ev_w] + (sq_extra if g == 0 else [])
                        p.op("scalar", lambda e, cb_=cb:
                             e.activation(HBF[:, SH + cb_ * T:SH + (cb_ + 2) * T],
                                          dst_bf[:, cb_ * T:(cb_ + 2) * T],
                                          AF.Square),
                             waits=sqw, inc=SA)
                        pair_r[g % 2].set([ev_w])
                        ev_last = ev_w
                    sq_w = (SA, SA.count)
                    mwaits = [sq_w, consts_w] + mask_reg.require()
                    for hh in range(16):
                        p.op("tensor", lambda e, h_=hh, st=(hh == 0), sp=(hh == 15):
                             e.matmul(PSB[0:16, maskcol * T:(maskcol + 1) * T],
                                      maskA[:, h_ * 16:(h_ + 1) * 16],
                                      HBF[:, SH + h_ * T:SH + (h_ + 1) * T],
                                      start=st, stop=sp, skip_group_check=True),
                             waits=mwaits if hh == 0 else (),
                             inc=ST if hh == 15 else None)
                    return (ST, ST.count), ev_last, pair_r

                ssq_w, _, qpair = proj_qkv("wq", wq_b, QBF, 0, [],
                                           prev_psa, prev_psc, rowb0)
                ck("Aq", b, ti)
                sskv_w, kv_ev, kvpair = proj_qkv(
                    "wkv", wkv_b, HBF, 1, [ssq_w],
                    qpair[0].require(), qpair[1].require(), rowb1)

                p.op("vector", lambda e:
                     e.reciprocal(SQF[:, 0:2 * T], PSB[0:16, 0:2 * T]),
                     waits=[ssq_w, sskv_w], inc=SV)
                rec_w = (SV, SV.count)
                p.op("scalar", lambda e:
                     e.activation(SQB[:, 0:T], SQF[:, 0:T], AF.Sqrt,
                                  scale=1.0 / float(HD) ** 4),
                     waits=[rec_w], inc=SA)
                p.op("scalar", lambda e:
                     e.activation(SQB[:, T:2 * T], SQF[:, T:2 * T], AF.Sqrt),
                     waits=[rec_w], inc=SA)
                rows_w = (SA, SA.count)
                ck("Arows", b, ti)

                pref_add_t = {}
                mixtr_w = sskv_w
                rB0 = Region(); rB0.set([rec_w])
                rA0 = Region(); rA0.set(qpair[0].require() + kvpair[0].require())
                rA1 = Region(); rA1.set(rA0.require())
                rC0 = Region(); rC0.set(qpair[1].require() + kvpair[1].require())
                rC1 = Region(); rC1.set(rC0.require())
                vh_w_head = []
                ktb_h = {}
                scan_h = {}
                red_h = {}
                kvn_aps = {}
                prodsem = {}
                vh_mul_h = {}
                vh_mm_h = {}
                qcb_h = {}
                qsb_h = {}
                ksb_h = {}

                for s_ in range(17):
                    if s_ < 16:
                        h = s_
                        rot = h % 2
                        p.op("tensor", lambda e, h_=h:
                             e.matmul(PSB[:, 0:T],
                                      maskB[:, h_ * 128:(h_ + 1) * 128],
                                      SQB[:, T:2 * T], start=True, stop=True,
                                      skip_group_check=True),
                             waits=[rows_w] + rB0.require(), inc=ST)
                        bckv_w = (ST, ST.count)
                        kvn = HBF[:, M_KVN + rot * T:M_KVN + (rot + 1) * T]
                        kvn_aps[h] = kvn
                        p.op("vector", lambda e, h_=h, o=kvn:
                             e.tensor_mul(o, HBF[:, h_ * T:(h_ + 1) * T],
                                          PSB[:, 0:T]),
                             waits=[bckv_w, red_h.get(h - 2)] +
                                   ([mixtr_w] if h < 2 else []), inc=SV)
                        rB0.set([(SV, SV.count)])
                        kt = Ctmp[:, rot * T:(rot + 1) * T]
                        p.op("vector", lambda e, o=kt, i=kvn, h_=h:
                             e.tensor_tensor_scan(o, onest[:], i,
                                                  PREF[:, h_:h_ + 1],
                                                  ALU.mult, ALU.add),
                             waits=[memset_w, pref_w.get(h), ktb_h.get(h - 2)],
                             inc=SV)
                        scan_h[h] = (SV, SV.count)
                        p.op("scalar", lambda e, h_=h, i=kt:
                             e.activation(PREF[:, h_:h_ + 1], i[:, T - 1:T],
                                          AF.Copy),
                             waits=[scan_h[h]], inc=SA)
                        pref_w[h] = (SA, SA.count)
                        ktb = HBF[:, M_KTB + rot * T:M_KTB + (rot + 1) * T]
                        p.op("scalar", lambda e, o=ktb, i=kt:
                             e.activation(o, i, AF.Copy),
                             waits=[scan_h[h]], inc=SA)
                        ktb_h[h] = (SA, SA.count)
                    if s_ >= 1:
                        h = s_ - 1
                        ktb = HBF[:, M_KTB + (h % 2) * T:M_KTB + (h % 2 + 1) * T]
                        p.op("tensor", lambda e, h_=h:
                             e.matmul(PSC[:, 0:T], cdft[:],
                                      QBF[:, h_ * T:(h_ + 1) * T],
                                      start=True, stop=True,
                                      skip_group_check=True),
                             waits=rC0.require() + ([mixtr_w] if h == 0 else []),
                             inc=ST)
                        qc_w = (ST, ST.count)
                        p.op("tensor", lambda e, h_=h:
                             e.matmul(PSC[:, T:2 * T], sdft[:],
                                      QBF[:, h_ * T:(h_ + 1) * T],
                                      start=True, stop=True,
                                      skip_group_check=True),
                             waits=rC1.require(), inc=ST)
                        qs_w = (ST, ST.count)
                        p.op("tensor", lambda e, i=ktb:
                             e.matmul(PSD[:, 0:T], cdft[:], i, start=True,
                                      stop=True, skip_group_check=True),
                             waits=[ktb_h[h]] + rD0.require(), inc=ST)
                        kc_w = (ST, ST.count)
                        p.op("tensor", lambda e, i=ktb:
                             e.matmul(PSD[:, T:2 * T], sdft[:], i, start=True,
                                      stop=True, skip_group_check=True),
                             waits=rD1.require(), inc=ST)
                        ks_w = (ST, ST.count)

                        dbase = M_DFT + (h % 2) * 4 * T
                        qcb = HBF[:, dbase:dbase + T]
                        qsb = HBF[:, dbase + T:dbase + 2 * T]
                        kcb = HBF[:, dbase + 2 * T:dbase + 3 * T]
                        ksb = HBF[:, dbase + 3 * T:dbase + 4 * T]
                        pg = prodsem.get(h - 2, {})
                        p.op("vector", lambda e, o=qcb:
                             e.tensor_copy(o, PSC[:, 0:T]),
                             waits=[qc_w, pg.get(3)], inc=SV)
                        rC0.set([(SV, SV.count)])
                        qcb_h[h] = (SV, SV.count)
                        p.op("vector", lambda e, o=kcb:
                             e.tensor_copy(o, PSD[:, 0:T]),
                             waits=[kc_w], inc=SV)
                        rD0.set([(SV, SV.count)])
                        kcb_h = (SV, SV.count)
                        p.op("scalar", lambda e, o=qsb:
                             e.activation(o, PSC[:, T:2 * T], AF.Copy),
                             waits=[qs_w, pg.get(2), pg.get(4)], inc=SA)
                        rC1.set([(SA, SA.count)])
                        qsb_h[h] = (SA, SA.count)
                        p.op("scalar", lambda e, o=ksb:
                             e.activation(o, PSD[:, T:2 * T], AF.Copy),
                             waits=[ks_w, pg.get(2), pg.get(3)], inc=SA)
                        rD1.set([(SA, SA.count)])
                        ksb_h[h] = (SA, SA.count)

                        pbase = M_P + (h % 2) * 4 * T
                        prods = [HBF[:, pbase + j * T:pbase + (j + 1) * T]
                                 for j in range(4)]
                        slot_free = vh_mm_h.get(h - 2)
                        ps_ = {}
                        p.op("vector", lambda e, o=prods[0], a_=qcb, b_=kcb:
                             e.tensor_mul(o, a_, b_),
                             waits=[slot_free], inc=SV)
                        ps_[1] = (SV, SV.count)
                        p.op("gpsimd", lambda e, o=prods[1], a_=qsb, b_=ksb:
                             e.tensor_mul(o, a_, b_),
                             waits=[qsb_h[h], ksb_h[h], slot_free], inc=SP)
                        ps_[2] = (SP, SP.count)
                        p.op("gpsimd", lambda e, o=prods[2], a_=qcb, b_=ksb:
                             e.tensor_mul(o, a_, b_),
                             waits=[qcb_h[h], ksb_h[h], slot_free], inc=SP)
                        ps_[3] = (SP, SP.count)
                        p.op("vector", lambda e, o=prods[3], a_=qsb, b_=kcb:
                             e.tensor_mul(o, a_, b_),
                             waits=[qsb_h[h], slot_free], inc=SV)
                        ps_[4] = (SV, SV.count)
                        prodsem[h] = ps_
                        lhs_list = [cdft, ncdft, sdft, sdft]
                        pws = [ps_[1], ps_[2], ps_[3], ps_[4]]
                        for j in range(4):
                            p.op("tensor", lambda e, l=lhs_list[j], r=prods[j],
                                 st=(j == 0), sp=(j == 3):
                                 e.matmul(PSA[:, 0:T], l[:], r, start=st,
                                          stop=sp, skip_group_check=True),
                                 waits=[pws[j]] +
                                       (rA0.require() if j == 0 else []),
                                 inc=ST if j == 3 else None)
                        vh_w = (ST, ST.count)
                        vh_mm_h[h] = vh_w
                        p.op("tensor", lambda e, h_=h:
                             e.matmul(PSA[:, T:2 * T],
                                      maskB[:, h_ * 128:(h_ + 1) * 128],
                                      SQB[:, 0:T], start=True, stop=True,
                                      skip_group_check=True),
                             waits=[rows_w] + rA1.require(), inc=ST)
                        bcq_w = (ST, ST.count)
                        sqqb = HBF[:, M_SQ + (h % 2) * T:M_SQ + (h % 2 + 1) * T]
                        p.op("scalar", lambda e, o=sqqb:
                             e.activation(o, PSA[:, T:2 * T], AF.Copy),
                             waits=[bcq_w, vh_mul_h.get(h - 2)], inc=SA)
                        sqqb_w = (SA, SA.count)
                        rA1.set([sqqb_w])
                        p.op("vector", lambda e, h_=h, s_a=sqqb:
                             e.tensor_mul(VHBF[:, h_ * T:(h_ + 1) * T],
                                          PSA[:, 0:T], s_a),
                             waits=[vh_w, sqqb_w], inc=SV)
                        rA0.set([(SV, SV.count)])
                        vh_mul_h[h] = (SV, SV.count)
                        vh_w_head.append((SV, SV.count))
                        ck(f"Ah{h}", b, ti)
                vh_done = (SV, SV.count)
                ck("A", b, ti)

                pair_r = [Region(), Region()]
                pair_r[0].set(rA0.require() + rA1.require())
                pair_r[1].set(rC0.require() + rC1.require())
                x2a_ws = []
                for g in range(OGD):
                    sl, w_w = load_strip("wo", wo_b, g, KD * 256)
                    ps = PSA if g % 2 == 0 else PSC
                    waits = [w_w] + pair_r[g % 2].require()
                    for k in range(KD):
                        if g < 2:
                            kw = (waits + [vh_w_head[k]]) if k == 0 else \
                                [vh_w_head[k]]
                        else:
                            kw = waits if k == 0 else ()
                        for oc in range(2):
                            lhsT = WS[:, sl * SLOTW + k * 256 + oc * 128:
                                      sl * SLOTW + k * 256 + (oc + 1) * 128]
                            p.op("tensor",
                                 lambda e, o=ps[:, oc * T:(oc + 1) * T],
                                 l=lhsT, r=VHBF[:, k * T:(k + 1) * T],
                                 st=(k == 0), sp=(k == KD - 1):
                                 e.matmul(o, l, r, start=st, stop=sp,
                                          skip_group_check=True),
                                 waits=kw if oc == 0 else (),
                                 inc=ST if (k == KD - 1 and oc == 1) else None)
                    mm_w = (ST, ST.count)
                    rSLOT[sl].set([mm_w])
                    cb = 2 * g
                    p.op("vector", lambda e, cb_=cb, ps_=ps:
                         e.tensor_add(A[:, cb_ * T:(cb_ + 2) * T], ps_[:, 0:1024],
                                      B[:, cb_ * T:(cb_ + 2) * T]),
                         waits=[mm_w, xf_w], inc=SV)
                    x2a_w = (SV, SV.count)
                    pair_r[g % 2].set([x2a_w])
                    p.op("scalar", lambda e, cb_=cb:
                         e.activation(HBF[:, cb_ * T:(cb_ + 2) * T],
                                      A[:, cb_ * T:(cb_ + 2) * T], AF.Copy),
                         waits=[x2a_w] + ([vh_done] if g == 0 else []), inc=SA)
                    p.op("scalar", lambda e, cb_=cb:
                         e.activation(HBF[:, SH + cb_ * T:SH + (cb_ + 2) * T],
                                      A[:, cb_ * T:(cb_ + 2) * T], AF.Square),
                         inc=SA)
                    w2 = (SA, SA.count)
                    x2a_ws.extend([w2, w2])

                rowb0.set(rB0.require())
                rowb1.set([rec_w])
                x1_ws, ln1_v = ln_rows_apply(
                    A, B, XBF, HBF, HBF, SH, PSD, rowb0, rowb1, rD0, rD1,
                    x2a_ws, None)
                rB_F32.set([])
                ck("B", b, ti)

                hprev = {}
                h_ws = []
                gpair = [Region(), Region()]
                gpair[0].set(pair_r[0].require())
                gpair[1].set(pair_r[1].require())
                upair = [Region(), Region()]
                upair[0].set(rowb0.require() + rowb1.require())
                upair[1].set([ln1_v])
                for g in range(OGI):
                    slg, wg_w = load_strip("wg", wg_b, g, KD * 256)
                    psg = PSA if g % 2 == 0 else PSC
                    waits = [wg_w] + gpair[g % 2].require()
                    for k in range(KD):
                        if g < 2:
                            kw = (waits + [x1_ws[k]]) if k == 0 else [x1_ws[k]]
                        else:
                            kw = waits if k == 0 else ()
                        for oc in range(2):
                            lhsT = WS[:, slg * SLOTW + k * 256 + oc * 128:
                                      slg * SLOTW + k * 256 + (oc + 1) * 128]
                            p.op("tensor",
                                 lambda e, o=psg[:, oc * T:(oc + 1) * T],
                                 l=lhsT, r=XBF[:, k * T:(k + 1) * T],
                                 st=(k == 0), sp=(k == KD - 1):
                                 e.matmul(o, l, r, start=st, stop=sp,
                                          skip_group_check=True),
                                 waits=kw if oc == 0 else (),
                                 inc=ST if (k == KD - 1 and oc == 1) else None)
                    g_w = (ST, ST.count)
                    rSLOT[slg].set([g_w])
                    slu, wu_w = load_strip("wu", wu_b, g, KD * 256)
                    psu = PSB if g % 2 == 0 else PSD
                    waitsu = [wu_w] + upair[g % 2].require()
                    for k in range(KD):
                        if g < 2:
                            kw = (waitsu + [x1_ws[k]]) if k == 0 else [x1_ws[k]]
                        else:
                            kw = waitsu if k == 0 else ()
                        for oc in range(2):
                            lhsT = WS[:, slu * SLOTW + k * 256 + oc * 128:
                                      slu * SLOTW + k * 256 + (oc + 1) * 128]
                            p.op("tensor",
                                 lambda e, o=psu[:, oc * T:(oc + 1) * T],
                                 l=lhsT, r=XBF[:, k * T:(k + 1) * T],
                                 st=(k == 0), sp=(k == KD - 1):
                                 e.matmul(o, l, r, start=st, stop=sp,
                                          skip_group_check=True),
                                 waits=kw if oc == 0 else (),
                                 inc=ST if (k == KD - 1 and oc == 1) else None)
                    u_w = (ST, ST.count)
                    rSLOT[slu].set([u_w])
                    for oc in range(2):
                        c = 2 * g + oc
                        rot = c % 2
                        sg_t = QBF[:, rot * T:(rot + 1) * T]
                        u_t = QBF[:, (2 + rot) * T:(3 + rot) * T]
                        hw = [hprev[rot]] if c >= 2 else []
                        p.op("scalar", lambda e, o=sg_t, oc_=oc, c_=c, ps_=psg:
                             e.activation(o, ps_[:, oc_ * T:(oc_ + 1) * T],
                                          AF.Silu, scale=gs[:, c_:c_ + 1]),
                             waits=[g_w] + hw, inc=SA)
                        p.op("scalar", lambda e, o=u_t, oc_=oc, c_=c, ps_=psu:
                             e.mul(o, ps_[:, oc_ * T:(oc_ + 1) * T],
                                   us[:, c_:c_ + 1]),
                             waits=[u_w], inc=SA)
                        su_w = (SA, SA.count)
                        p.op("vector", lambda e, c_=c, a_=sg_t, b_=u_t:
                             e.tensor_mul(HBF[:, c_ * T:(c_ + 1) * T], a_, b_),
                             waits=[su_w], inc=SV)
                        hprev[rot] = (SV, SV.count)
                        h_ws.append((SV, SV.count))
                    gpair[g % 2].set([(SA, SA.count)])
                    upair[g % 2].set([(SA, SA.count)])
                rXBF.set([(ST, ST.count)])
                ck("C", b, ti)

                bank_regs = [Region(), Region(), Region(), Region()]
                bank_regs[0].set(gpair[0].require())
                bank_regs[1].set(gpair[0].require())
                bank_regs[2].set(gpair[1].require())
                bank_regs[3].set(gpair[1].require())
                bank_aps = [(PSA, 0), (PSA, 1), (PSC, 0), (PSC, 1)]
                x2b_ws = []
                rQ = [Region(), Region(), Region(), Region()]
                rQ[0].set(rSLOT[0].require())
                rQ[1].set(rSLOT[0].require())
                rQ[2].set(rSLOT[1].require())
                rQ[3].set(rSLOT[1].require())
                qbase = [0, 2048, SLOTW, SLOTW + 2048]
                qctr = [0]
                qmap = {}

                def load_q(g_, qd):
                    ss = qctr[0] % 4
                    qctr[0] += 1
                    dst = WS[:, qbase[ss]:qbase[ss] + KH * 128]
                    waits = [cast_at[("wd", g_)]] + rQ[ss].require()
                    p.op("sync", lambda e, s=wd_b.ap()[g_, qd], d=dst:
                         e.dma_start(out=d, in_=s),
                         waits=waits, inc=SW, inc_n=16)
                    qmap[(g_, qd)] = (ss, (SW, SW.count))

                for qd in range(4):
                    load_q(0, qd)
                for g in range(OGDN):
                    ps, half = bank_aps[g % 4]
                    waits = [qmap[(g, 0)][1]] + bank_regs[g % 4].require()
                    for k in range(KI):
                        qd = k // KH
                        lastq = (k % KH == KH - 1)
                        ss, q_w = qmap[(g, qd)]
                        lhsT = WS[:, qbase[ss] + (k % KH) * 128:
                                  qbase[ss] + (k % KH + 1) * 128]
                        kw = list(waits) if k == 0 else \
                            ([q_w] if k % KH == 0 else [])
                        if g < 4:
                            kw = kw + [h_ws[k]]
                        p.op("tensor",
                             lambda e, o=ps[:, half * T:(half + 1) * T],
                             l=lhsT, k_=k, st=(k == 0), sp=(k == KI - 1):
                             e.matmul(o, l, HBF[:, k_ * T:(k_ + 1) * T],
                                      start=st, stop=sp, skip_group_check=True),
                             waits=kw, inc=ST if lastq else None)
                        if lastq:
                            rQ[ss].set([(ST, ST.count)])
                            nxt = g * 4 + qd + 4
                            if nxt < OGDN * 4:
                                load_q(nxt // 4, nxt % 4)
                    mm_w = (ST, ST.count)
                    c = g
                    p.op("vector", lambda e, c_=c, o=A[:, c * T:(c + 1) * T],
                         i=ps[:, half * T:(half + 1) * T]:
                         e.scalar_tensor_tensor(o, i, ds[:, c_:c_ + 1],
                                                B[:, c_ * T:(c_ + 1) * T],
                                                ALU.mult, ALU.add),
                         waits=[mm_w], inc=SV)
                    x2b_w = (SV, SV.count)
                    bank_regs[g % 4].set([x2b_w])
                    p.op("scalar", lambda e, c_=c:
                         e.activation(VHBF[:, c_ * T:(c_ + 1) * T],
                                      A[:, c_ * T:(c_ + 1) * T], AF.Copy),
                         waits=[x2b_w], inc=SA)
                    p.op("scalar", lambda e, c_=c:
                         e.activation(QBF[:, c_ * T:(c_ + 1) * T],
                                      A[:, c_ * T:(c_ + 1) * T], AF.Square),
                         waits=[hprev[0], hprev[1]] if g == 0 else (), inc=SA)
                    w2 = (SA, SA.count)
                    x2b_ws.append(w2)

                rowb0.set(upair[0].require())
                rowb1.set(upair[0].require())
                rD0.set(upair[1].require())
                rD1.set(upair[1].require())
                _, ln2_v = ln_rows_apply(
                    A, B, None, VHBF, QBF, 0, PSD, rowb0, rowb1, rD0, rD1,
                    x2b_ws, (b, s0))
                store_done = (SS, SS.count)
                rB_F32.set([store_done])
                rSLOT[0].set([mm_w])
                rSLOT[1].set([mm_w])
                prev_psa = bank_regs[0].require() + bank_regs[1].require()
                prev_psc = bank_regs[2].require() + bank_regs[3].require()
                ck("D", b, ti)
                p.barrier(only=("s_t", "s_v", "s_a", "s_p"))
    except _StopEmit:
        pass
    if _dbg:
        emit_dbg_taps()

    def fin(eng):
        eng.wait_ge(SS.h, SS.count)
    p.streams["scalar"].append(fin)
    p.finalize()
    return nc


def _prep_host(inputs):
    x = np.ascontiguousarray(np.asarray(inputs["L0_attn_x"], dtype=np.float32))
    wq = np.asarray(inputs["L0_attn_W_q"], dtype=np.float32)
    wkv = np.asarray(inputs["L0_attn_W_kv"], dtype=np.float32)
    wo = np.asarray(inputs["L0_attn_W_o"], dtype=np.float32)
    gw = np.asarray(inputs["L0_ffn_gate_w"], dtype=np.float32)
    uw = np.asarray(inputs["L0_ffn_up_w"], dtype=np.float32)
    dw = np.asarray(inputs["L0_ffn_down_w"], dtype=np.float32)
    gsv = np.asarray(inputs["L0_ffn_gate_s"], dtype=np.float32)
    usv = np.asarray(inputs["L0_ffn_up_s"], dtype=np.float32)
    dsv = np.asarray(inputs["L0_ffn_down_s"], dtype=np.float32)

    def strip_d(w, nog):
        return np.ascontiguousarray(
            w.T.reshape(KD, 128, nog, 256).transpose(2, 1, 0, 3).reshape(
                nog, 128, KD * 256))

    wd_s = np.ascontiguousarray(
        dw.T.reshape(4, KH, 128, OGDN, 128).transpose(3, 0, 2, 1, 4).reshape(
            OGDN, 4, 128, KH * 128))

    m = np.arange(128)
    ang = 2.0 * np.pi * np.outer(m, m) / 128.0
    cdft = np.cos(ang).astype(np.float32).astype(ml_dtypes.bfloat16)
    sdft = np.sin(ang).astype(np.float32).astype(ml_dtypes.bfloat16)
    ncdft = (-np.cos(ang)).astype(np.float32).astype(ml_dtypes.bfloat16)
    maskA = np.zeros((128, 16, 16), dtype=ml_dtypes.bfloat16)
    for h in range(16):
        maskA[:, h, h] = 1
    maskB = np.zeros((16, 16, 128), dtype=ml_dtypes.bfloat16)
    for h in range(16):
        maskB[h, h, :] = 1

    shared = dict(
        wq_f=strip_d(wq, OGD), wkv_f=strip_d(wkv, OGD), wo_f=strip_d(wo, OGD),
        wg_f=strip_d(gw, OGI), wu_f=strip_d(uw, OGI), wd_f=wd_s,
        gs_in=np.ascontiguousarray(gsv.reshape(KI, 128).T),
        us_in=np.ascontiguousarray(usv.reshape(KI, 128).T),
        ds_in=np.ascontiguousarray(dsv.reshape(KD, 128).T),
        cdft_in=cdft, sdft_in=sdft, ncdft_in=ncdft,
        maskA_in=np.ascontiguousarray(maskA.reshape(128, 256)),
        maskB_in=np.ascontiguousarray(maskB.reshape(16, 2048)),
    )
    in_maps = []
    for i in range(NCORES):
        xc = np.ascontiguousarray(x[BL * i:BL * (i + 1)].transpose(0, 2, 1))
        im = dict(shared)
        im["x_fm"] = xc
        in_maps.append(im)
    return in_maps


def kernel(**inputs):
    in_maps = _prep_host(inputs)
    nc = build_nc()
    res = run_bass_kernel_spmd(nc, in_maps, core_ids=list(range(NCORES)))
    out = np.empty((16, S, D), dtype=np.float32)
    for i in range(NCORES):
        o = np.asarray(res.results[i]["out_fm"])
        out[BL * i:BL * (i + 1)] = o.transpose(0, 2, 1)
    return out


# version 17
# speedup vs baseline: 1.1775x; 1.0323x over previous
import numpy as np
import ml_dtypes
import concourse.bass as bass
import concourse.mybir as mybir
from concourse.bass_utils import run_bass_kernel_spmd

F32 = mybir.dt.float32
BF16 = mybir.dt.bfloat16
AF = mybir.ActivationFunctionType
ALU = mybir.AluOpType
AX = mybir.AxisListType

D = 2048
S = 2048
H = 16
HD = 128
INTER = 5632
T = 512
NT = S // T
BL = 2
KD = D // 128
KI = INTER // 128
KH = KI // 4
OGD = 8
OGI = 22
OGDN = 16
NCORES = 8
EPS = 1e-5
SLOTW = 4096


class Sem:
    def __init__(self, nc, name):
        self.h = nc.alloc_semaphore(name)
        self.count = 0
        self.name = name


class Region:
    def __init__(self):
        self.req = []

    def require(self):
        return list(self.req)

    def set(self, reqs):
        self.req = [r for r in reqs if r is not None]


class Plan:
    ENGINES = ("tensor", "vector", "scalar", "sync", "gpsimd")

    def __init__(self, nc):
        self.nc = nc
        self.streams = {e: [] for e in self.ENGINES}
        self.sems = []

    def sem(self, name):
        s = Sem(self.nc, name)
        self.sems.append(s)
        return s

    def op(self, engine, build, waits=(), inc=None, inc_n=1):
        wmap = {}
        for w in waits:
            if w is None:
                continue
            s, v = w
            assert v <= s.count, f"wait on {s.name} for {v} > emitted {s.count}"
            if v > 0 and (s.h not in wmap or wmap[s.h] < v):
                wmap[s.h] = v
        if inc is not None:
            inc.count += inc_n
            inc_h = inc.h
        else:
            inc_h = None
        wlist = list(wmap.items())

        def fn(eng, _wl=wlist, _build=build, _ih=inc_h, _n=inc_n):
            for h, v in _wl:
                eng.wait_ge(h, v)
            inst = _build(eng)
            if _ih is not None:
                inst.then_inc(_ih, _n)
        self.streams[engine].append(fn)

    def barrier(self, engines=("tensor", "vector", "scalar", "gpsimd"),
                only=None):
        snap = [(s.h, s.count) for s in self.sems
                if s.count > 0 and (only is None or s.name in only)]
        for e in engines:
            def fn(eng, _snap=list(snap)):
                for h, v in _snap:
                    eng.wait_ge(h, v)
            self.streams[e].append(fn)

    def finalize(self):
        nc = self.nc
        with nc.Block() as block:
            block.tensor(lambda t: [f(t) for f in self.streams["tensor"]])
            block.vector(lambda v: [f(v) for f in self.streams["vector"]])
            block.scalar(lambda s: [f(s) for f in self.streams["scalar"]])
            block.sync(lambda s: [f(s) for f in self.streams["sync"]])
            block.gpsimd(lambda g: [f(g) for f in self.streams["gpsimd"]])


class _StopEmit(Exception):
    pass


def build_nc(_dbg=None):
    nc = bass.Bass("TRN2", target_bir_lowering=False)

    x_fm = nc.dram_tensor("x_fm", [BL, D, S], F32, kind="ExternalInput")
    wq_f = nc.dram_tensor("wq_f", [OGD, 128, KD * 256], F32, kind="ExternalInput")
    wkv_f = nc.dram_tensor("wkv_f", [OGD, 128, KD * 256], F32, kind="ExternalInput")
    wo_f = nc.dram_tensor("wo_f", [OGD, 128, KD * 256], F32, kind="ExternalInput")
    wg_f = nc.dram_tensor("wg_f", [OGI, 128, KD * 256], F32, kind="ExternalInput")
    wu_f = nc.dram_tensor("wu_f", [OGI, 128, KD * 256], F32, kind="ExternalInput")
    wd_f = nc.dram_tensor("wd_f", [OGDN, 4, 128, KH * 128], F32, kind="ExternalInput")
    gs_in = nc.dram_tensor("gs_in", [128, KI], F32, kind="ExternalInput")
    us_in = nc.dram_tensor("us_in", [128, KI], F32, kind="ExternalInput")
    ds_in = nc.dram_tensor("ds_in", [128, KD], F32, kind="ExternalInput")
    cdft_in = nc.dram_tensor("cdft_in", [128, 128], BF16, kind="ExternalInput")
    sdft_in = nc.dram_tensor("sdft_in", [128, 128], BF16, kind="ExternalInput")
    ncdft_in = nc.dram_tensor("ncdft_in", [128, 128], BF16, kind="ExternalInput")
    maskA_in = nc.dram_tensor("maskA_in", [128, 256], BF16, kind="ExternalInput")
    maskB_in = nc.dram_tensor("maskB_in", [16, 2048], BF16, kind="ExternalInput")
    out_fm = nc.dram_tensor("out_fm", [BL, D, S], F32, kind="ExternalOutput")
    dbg_out = None
    if _dbg:
        dbg_t = {
            "dbg_qbf": ([128, 8192], BF16), "dbg_vh": ([128, 8192], BF16),
            "dbg_xbf": ([128, 8192], BF16), "dbg_sqf": ([16, 1024], F32),
            "dbg_a": ([128, 8192], F32), "dbg_b": ([128, 8192], F32),
            "dbg_hbf": ([128, 22528], BF16), "dbg_pref": ([128, 16], F32),
        }
        dbg_out = {k: nc.dram_tensor(k, sh, dt, kind="ExternalOutput")
                   for k, (sh, dt) in dbg_t.items()}

    wq_b = nc.dram_tensor("wq_b", [OGD, 128, KD * 256], BF16, kind="Internal")
    wkv_b = nc.dram_tensor("wkv_b", [OGD, 128, KD * 256], BF16, kind="Internal")
    wo_b = nc.dram_tensor("wo_b", [OGD, 128, KD * 256], BF16, kind="Internal")
    wg_b = nc.dram_tensor("wg_b", [OGI, 128, KD * 256], BF16, kind="Internal")
    wu_b = nc.dram_tensor("wu_b", [OGI, 128, KD * 256], BF16, kind="Internal")
    wd_b = nc.dram_tensor("wd_b", [OGDN, 4, 128, KH * 128], BF16, kind="Internal")

    A = nc.alloc_sbuf_tensor("buf_a", [128, KD * T], F32)
    B = nc.alloc_sbuf_tensor("buf_b", [128, KD * T], F32)
    Ctmp = nc.alloc_sbuf_tensor("buf_c", [128, 2 * T], F32)
    SQF = nc.alloc_sbuf_tensor("sq_f", [16, 2 * T], F32)
    SQB = nc.alloc_sbuf_tensor("sq_b", [16, 2 * T], BF16)
    ROWSF = SQF
    ROWSB = SQB
    PREF = nc.alloc_sbuf_tensor("prefix", [128, 16], F32)
    PREFT = nc.alloc_sbuf_tensor("preft", [128, 4], F32)
    XBF = nc.alloc_sbuf_tensor("x_bf", [128, KD * T], BF16)
    QBF = nc.alloc_sbuf_tensor("q_bf", [128, KD * T], BF16)
    VHBF = nc.alloc_sbuf_tensor("vh_bf", [128, KD * T], BF16)
    HBF = nc.alloc_sbuf_tensor("h_bf", [128, KI * T], BF16)
    WS = nc.alloc_sbuf_tensor("wslot", [128, 2 * SLOTW], BF16)
    cdft = nc.alloc_sbuf_tensor("cdft", [128, 128], BF16)
    sdft = nc.alloc_sbuf_tensor("sdft", [128, 128], BF16)
    ncdft = nc.alloc_sbuf_tensor("ncdft", [128, 128], BF16)
    maskA = nc.alloc_sbuf_tensor("maskA", [128, 256], BF16)
    maskB = nc.alloc_sbuf_tensor("maskB", [16, 2048], BF16)
    gs = nc.alloc_sbuf_tensor("gs", [128, KI], F32)
    us = nc.alloc_sbuf_tensor("us", [128, KI], F32)
    ds = nc.alloc_sbuf_tensor("ds", [128, KD], F32)
    ones1 = nc.alloc_sbuf_tensor("ones1", [1, 128], BF16)
    onescol = nc.alloc_sbuf_tensor("onescol", [128, 1], BF16)
    onest = nc.alloc_sbuf_tensor("onest", [128, T], BF16)

    SH = KD * T
    M_QC = SH + 0
    M_QS = SH + T
    M_KC = SH + 2 * T
    M_KS = SH + 3 * T
    M_P = SH + 4 * T
    M_KVN = SH + 12 * T
    M_KTB = SH + 14 * T
    M_SQ = SH + 16 * T
    M_DFT = SH + 18 * T

    PSA = nc.alloc_psum_tensor("psa", [128, 1024], F32)
    PSB = nc.alloc_psum_tensor("psb", [128, 1024], F32)
    PSC = nc.alloc_psum_tensor("psc", [128, 1024], F32)
    PSD = nc.alloc_psum_tensor("psd", [128, 1024], F32)

    p = Plan(nc)
    ST = p.sem("s_t")
    SV = p.sem("s_v")
    SA = p.sem("s_a")
    SW = p.sem("s_w")
    SC = p.sem("s_c")
    SG = p.sem("s_g")
    SS = p.sem("s_s")
    SP = p.sem("s_p")

    rSLOT = [Region(), Region()]
    rXBF = Region()
    rB_F32 = Region()

    for dst, src in ((cdft, cdft_in), (sdft, sdft_in), (ncdft, ncdft_in),
                     (maskA, maskA_in), (maskB, maskB_in),
                     (gs, gs_in), (us, us_in), (ds, ds_in)):
        p.op("sync", lambda e, d=dst, s=src: e.dma_start(out=d[:], in_=s.ap()),
             inc=SW, inc_n=16)
    consts_w = (SW, SW.count)
    SM = p.sem("s_m")
    p.op("gpsimd", lambda e: e.memset(ones1[:], 1.0), inc=SM)
    p.op("gpsimd", lambda e: e.memset(onescol[:], 1.0), inc=SM)
    p.op("gpsimd", lambda e: e.memset(onest[:], 1.0), inc=SM)
    memset_w = (SM, SM.count)

    def emit_xbf_load(b, s0, waits):
        src = x_fm.ap()[b, :, s0:s0 + T].rearrange("(c p) t -> p c t", p=128)
        dst = XBF[:].rearrange("p (c t) -> p c t", c=KD)
        p.op("gpsimd", lambda e, s=src, d=dst: e.dma_start(out=d, in_=s),
             waits=waits, inc=SG, inc_n=16)

    def emit_xf_load(b, s0, waits):
        src = x_fm.ap()[b, :, s0:s0 + T].rearrange("(c p) t -> p c t", p=128)
        dst = B[:].rearrange("p (c t) -> p c t", c=KD)
        p.op("scalar", lambda e, s=src, d=dst: e.dma_start(out=d, in_=s),
             waits=waits, inc=SS, inc_n=16)

    emit_xbf_load(0, 0, [])
    emit_xf_load(0, 0, [])

    cast_order = []
    for name, fsrc, bdst, nog in (("wq", wq_f, wq_b, OGD), ("wkv", wkv_f, wkv_b, OGD),
                                  ("wo", wo_f, wo_b, OGD)):
        for g in range(nog):
            cast_order.append((name, g, fsrc, bdst))
    for g in range(OGI):
        cast_order.append(("wg", g, wg_f, wg_b))
        cast_order.append(("wu", g, wu_f, wu_b))
    for g in range(OGDN):
        cast_order.append(("wd", g, wd_f, wd_b))
    cast_at = {}
    for (name, g, fsrc, bdst) in cast_order:
        p.op("gpsimd", lambda e, s=fsrc, d=bdst, gg=g:
             e.dma_start(out=d.ap()[gg], in_=s.ap()[gg]), inc=SC, inc_n=16)
        cast_at[(name, g)] = (SC, SC.count)

    strip_ctr = [0]

    def load_strip(name, bdst, g, width, qd=None):
        sl = strip_ctr[0] % 2
        strip_ctr[0] += 1
        src = bdst.ap()[g] if qd is None else bdst.ap()[g, qd]
        dst = WS[:, sl * SLOTW:sl * SLOTW + width]
        waits = [cast_at[(name, g)]] + rSLOT[sl].require()
        p.op("sync", lambda e, s=src, d=dst: e.dma_start(out=d, in_=s),
             waits=waits, inc=SW, inc_n=16)
        return sl, (SW, SW.count)

    def emit_dbg_taps():
        p.barrier(engines=("scalar",))
        pairs = [("dbg_qbf", QBF[:]), ("dbg_vh", VHBF[:]), ("dbg_xbf", XBF[:]),
                 ("dbg_sqf", SQF[:]), ("dbg_a", A[:]), ("dbg_b", B[:]),
                 ("dbg_hbf", HBF[:]), ("dbg_pref", PREF[:])]
        for k, ap in pairs:
            p.op("scalar", lambda e, d=dbg_out[k], a_=ap:
                 e.dma_start(out=d.ap(), in_=a_), inc=SS, inc_n=16)
        for bb in range(BL):
            for ss0 in range(0, S, T):
                p.op("scalar", lambda e, bb_=bb, s_=ss0:
                     e.dma_start(out=out_fm.ap()[bb_, :, s_:s_ + T].rearrange(
                         "(c p) t -> p c t", p=128),
                         in_=B[:].rearrange("p (c t) -> p c t", c=KD)),
                     inc=SS, inc_n=16)

    def ck(tag, b, ti):
        if _dbg and _dbg.get("stop") == tag and \
                (b * NT + ti) == _dbg.get("ntiles", 8) - 1:
            raise _StopEmit()

    def ln_rows_apply(src_f32, dst_f32, dst_bf, src_bf_buf, src_sq_buf, sq_off,
                      bc_ps, rowb0, rowb1, bcb0, bcb1, row_ws, store_tile):
        for c in range(KD):
            w = [row_ws[c], memset_w] + (rowb0.require() if c == 0 else [])
            p.op("tensor", lambda e, c_=c, st=(c == 0), sp=(c == KD - 1):
                 e.matmul(PSB[0:1, 0:T], onescol[:],
                          src_bf_buf[:, c_ * T:(c_ + 1) * T],
                          start=st, stop=sp, skip_group_check=True),
                 waits=w, inc=ST if c == KD - 1 else None)
        for c in range(KD):
            w = [row_ws[c]] + (rowb1.require() if c == 0 else [])
            p.op("tensor", lambda e, c_=c, st=(c == 0), sp=(c == KD - 1):
                 e.matmul(PSB[0:1, T:2 * T], onescol[:],
                          src_sq_buf[:, sq_off + c_ * T:sq_off + (c_ + 1) * T],
                          start=st, stop=sp, skip_group_check=True),
                 waits=w, inc=ST if c == KD - 1 else None)
        rows_mm = (ST, ST.count)
        p.op("vector", lambda e:
             e.tensor_scalar_mul(ROWSF[0:1, 0:T], PSB[0:1, 0:T], 1.0 / D),
             waits=[rows_mm], inc=SV)
        p.op("vector", lambda e:
             e.tensor_mul(ROWSF[0:1, T:2 * T], ROWSF[0:1, 0:T],
                          ROWSF[0:1, 0:T]), inc=SV)
        p.op("vector", lambda e:
             e.scalar_tensor_tensor(ROWSF[0:1, T:2 * T], PSB[0:1, T:2 * T],
                                    1.0 / D, ROWSF[0:1, T:2 * T],
                                    ALU.mult, ALU.subtract), inc=SV)
        p.op("vector", lambda e:
             e.tensor_scalar_add(ROWSF[0:1, T:2 * T], ROWSF[0:1, T:2 * T], EPS),
             inc=SV)
        p.op("vector", lambda e:
             e.reciprocal(ROWSF[0:1, T:2 * T], ROWSF[0:1, T:2 * T]), inc=SV)
        rowsv = (SV, SV.count)
        rowb0.set([rowsv])
        rowb1.set([rowsv])
        p.op("scalar", lambda e:
             e.activation(ROWSB[0:1, T:2 * T], ROWSF[0:1, T:2 * T], AF.Sqrt),
             waits=[rowsv], inc=SA)
        p.op("scalar", lambda e:
             e.activation(ROWSB[0:1, 0:T], ROWSF[0:1, 0:T], AF.Copy),
             waits=[rowsv], inc=SA)
        rowsb = (SA, SA.count)
        p.op("tensor", lambda e:
             e.matmul(bc_ps[:, 0:T], ones1[:], ROWSB[0:1, 0:T],
                      start=True, stop=True, skip_group_check=True),
             waits=[rowsb, memset_w] + bcb0.require(), inc=ST)
        p.op("tensor", lambda e:
             e.matmul(bc_ps[:, T:2 * T], ones1[:], ROWSB[0:1, T:2 * T],
                      start=True, stop=True, skip_group_check=True),
             waits=[rowsb] + bcb1.require(), inc=ST)
        bc_w = (ST, ST.count)
        apply_ws = []
        for c in range(KD):
            tslot = Ctmp[:, (c % 2) * T:(c % 2 + 1) * T]
            p.op("vector", lambda e, c_=c, ts=tslot:
                 e.tensor_sub(ts, src_f32[:, c_ * T:(c_ + 1) * T], bc_ps[:, 0:T]),
                 waits=[bc_w] if c == 0 else (), inc=SV)
            p.op("vector", lambda e, c_=c, ts=tslot:
                 e.tensor_mul(dst_f32[:, c_ * T:(c_ + 1) * T], ts,
                              bc_ps[:, T:2 * T]), inc=SV)
            app_w = (SV, SV.count)
            if dst_bf is not None:
                p.op("scalar", lambda e, c_=c:
                     e.activation(dst_bf[:, c_ * T:(c_ + 1) * T],
                                  dst_f32[:, c_ * T:(c_ + 1) * T], AF.Copy),
                     waits=[app_w], inc=SA)
                apply_ws.append((SA, SA.count))
            if store_tile is not None:
                bb, ss0 = store_tile
                p.op("scalar", lambda e, c_=c:
                     e.dma_start(out=out_fm.ap()[bb, c_ * 128:(c_ + 1) * 128,
                                                 ss0:ss0 + T],
                                 in_=dst_f32[:, c_ * T:(c_ + 1) * T]),
                     waits=[app_w], inc=SS, inc_n=16)
        lastv = (SV, SV.count)
        bcb0.set([lastv])
        bcb1.set([lastv])
        return apply_ws, lastv

    first_tile = True
    pref_w = {}
    rowb0 = Region()
    rowb1 = Region()
    rD0 = Region()
    rD1 = Region()
    prev_psa = []
    prev_psc = []
    try:
        for b in range(BL):
            for ti in range(NT):
                s0 = ti * T
                if not first_tile:
                    emit_xbf_load(b, s0, rXBF.require())
                    emit_xf_load(b, s0, rB_F32.require())
                first_tile = False
                xbf_w = (SG, SG.count)
                xf_w = (SS, SS.count)

                if ti == 0:
                    p.op("vector", lambda e: e.memset(PREF[:], 0.0),
                         waits=[(SA, SA.count)], inc=SV)
                    pref_w = {}

                def proj_qkv(name, bdst, dst_bf, maskcol, sq_extra,
                             seed0, seed1, mask_reg):
                    pair_r = [Region(), Region()]
                    pair_r[0].set(seed0)
                    pair_r[1].set(seed1)
                    ev_last = None
                    for g in range(OGD):
                        sl, w_w = load_strip(name, bdst, g, KD * 256)
                        ps = PSA if g % 2 == 0 else PSC
                        waits = [w_w, xbf_w] + pair_r[g % 2].require()
                        if g == 0:
                            waits += [consts_w]
                        for k in range(KD):
                            for oc in range(2):
                                lhsT = WS[:, sl * SLOTW + k * 256 + oc * 128:
                                          sl * SLOTW + k * 256 + (oc + 1) * 128]
                                p.op("tensor",
                                     lambda e, o=ps[:, oc * T:(oc + 1) * T],
                                     l=lhsT, r=XBF[:, k * T:(k + 1) * T],
                                     st=(k == 0), sp=(k == KD - 1):
                                     e.matmul(o, l, r, start=st, stop=sp,
                                              skip_group_check=True),
                                     waits=waits if (k == 0 and oc == 0) else (),
                                     inc=ST if (k == KD - 1 and oc == 1) else None)
                        mm_w = (ST, ST.count)
                        rSLOT[sl].set([mm_w])
                        cb = 2 * g
                        p.op("vector", lambda e, cb_=cb, ps_=ps:
                             e.tensor_copy(dst_bf[:, cb_ * T:(cb_ + 2) * T],
                                           ps_[:, 0:1024]),
                             waits=[mm_w], inc=SV)
                        ev_w = (SV, SV.count)
                        sqw = [ev_w] + (sq_extra if g == 0 else [])
                        p.op("scalar", lambda e, cb_=cb:
                             e.activation(HBF[:, SH + cb_ * T:SH + (cb_ + 2) * T],
                                          dst_bf[:, cb_ * T:(cb_ + 2) * T],
                                          AF.Square),
                             waits=sqw, inc=SA)
                        pair_r[g % 2].set([ev_w])
                        ev_last = ev_w
                    sq_w = (SA, SA.count)
                    mwaits = [sq_w, consts_w] + mask_reg.require()
                    for hh in range(16):
                        p.op("tensor", lambda e, h_=hh, st=(hh == 0), sp=(hh == 15):
                             e.matmul(PSB[0:16, maskcol * T:(maskcol + 1) * T],
                                      maskA[:, h_ * 16:(h_ + 1) * 16],
                                      HBF[:, SH + h_ * T:SH + (h_ + 1) * T],
                                      start=st, stop=sp, skip_group_check=True),
                             waits=mwaits if hh == 0 else (),
                             inc=ST if hh == 15 else None)
                    return (ST, ST.count), ev_last, pair_r

                ssq_w, _, qpair = proj_qkv("wq", wq_b, QBF, 0, [],
                                           prev_psa, prev_psc, rowb0)
                ck("Aq", b, ti)
                sskv_w, kv_ev, kvpair = proj_qkv(
                    "wkv", wkv_b, HBF, 1, [ssq_w],
                    qpair[0].require(), qpair[1].require(), rowb1)

                p.op("vector", lambda e:
                     e.reciprocal(SQF[:, 0:2 * T], PSB[0:16, 0:2 * T]),
                     waits=[ssq_w, sskv_w], inc=SV)
                rec_w = (SV, SV.count)
                p.op("scalar", lambda e:
                     e.activation(SQB[:, 0:T], SQF[:, 0:T], AF.Sqrt,
                                  scale=1.0 / float(HD) ** 4),
                     waits=[rec_w], inc=SA)
                p.op("scalar", lambda e:
                     e.activation(SQB[:, T:2 * T], SQF[:, T:2 * T], AF.Sqrt),
                     waits=[rec_w], inc=SA)
                rows_w = (SA, SA.count)
                ck("Arows", b, ti)

                pref_add_t = {}
                mixtr_w = sskv_w
                rB0 = Region(); rB0.set([rec_w])
                rA0 = Region(); rA0.set(qpair[0].require() + kvpair[0].require())
                rA1 = Region(); rA1.set(rA0.require())
                rC0 = Region(); rC0.set(qpair[1].require() + kvpair[1].require())
                rC1 = Region(); rC1.set(rC0.require())
                vh_w_head = []
                ktb_h = {}
                scan_h = {}
                red_h = {}
                kvn_aps = {}
                prodsem = {}
                vh_mul_h = {}
                vh_mm_h = {}
                qcb_h = {}
                qsb_h = {}
                ksb_h = {}

                for s_ in range(17):
                    if s_ < 16:
                        h = s_
                        rot = h % 2
                        p.op("tensor", lambda e, h_=h:
                             e.matmul(PSB[:, 0:T],
                                      maskB[:, h_ * 128:(h_ + 1) * 128],
                                      SQB[:, T:2 * T], start=True, stop=True,
                                      skip_group_check=True),
                             waits=[rows_w] + rB0.require(), inc=ST)
                        bckv_w = (ST, ST.count)
                        kvn = HBF[:, M_KVN + rot * T:M_KVN + (rot + 1) * T]
                        kvn_aps[h] = kvn
                        p.op("vector", lambda e, h_=h, o=kvn:
                             e.tensor_mul(o, HBF[:, h_ * T:(h_ + 1) * T],
                                          PSB[:, 0:T]),
                             waits=[bckv_w, red_h.get(h - 2)] +
                                   ([mixtr_w] if h < 2 else []), inc=SV)
                        rB0.set([(SV, SV.count)])
                        kt = Ctmp[:, rot * T:(rot + 1) * T]
                        p.op("vector", lambda e, o=kt, i=kvn, h_=h:
                             e.tensor_tensor_scan(o, onest[:], i,
                                                  PREF[:, h_:h_ + 1],
                                                  ALU.mult, ALU.add),
                             waits=[memset_w, pref_w.get(h), ktb_h.get(h - 2)],
                             inc=SV)
                        scan_h[h] = (SV, SV.count)
                        p.op("vector", lambda e, i=kvn, h_=h:
                             e.reduce_sum(PREFT[:, h_ % 4:h_ % 4 + 1], i,
                                          axis=AX.X),
                             waits=[pref_add_t.get(h - 4)], inc=SV)
                        red_h[h] = (SV, SV.count)
                        p.op("scalar", lambda e, h_=h:
                             e.add(PREF[:, h_:h_ + 1], PREF[:, h_:h_ + 1],
                                   PREFT[:, h_ % 4:h_ % 4 + 1]),
                             waits=[red_h[h]], inc=SA)
                        pref_w[h] = (SA, SA.count)
                        pref_add_t[h] = (SA, SA.count)
                        ktb = HBF[:, M_KTB + rot * T:M_KTB + (rot + 1) * T]
                        p.op("scalar", lambda e, o=ktb, i=kt:
                             e.activation(o, i, AF.Copy),
                             waits=[scan_h[h]], inc=SA)
                        ktb_h[h] = (SA, SA.count)
                    if s_ >= 1:
                        h = s_ - 1
                        ktb = HBF[:, M_KTB + (h % 2) * T:M_KTB + (h % 2 + 1) * T]
                        p.op("tensor", lambda e, h_=h:
                             e.matmul(PSC[:, 0:T], cdft[:],
                                      QBF[:, h_ * T:(h_ + 1) * T],
                                      start=True, stop=True,
                                      skip_group_check=True),
                             waits=rC0.require() + ([mixtr_w] if h == 0 else []),
                             inc=ST)
                        qc_w = (ST, ST.count)
                        p.op("tensor", lambda e, h_=h:
                             e.matmul(PSC[:, T:2 * T], sdft[:],
                                      QBF[:, h_ * T:(h_ + 1) * T],
                                      start=True, stop=True,
                                      skip_group_check=True),
                             waits=rC1.require(), inc=ST)
                        qs_w = (ST, ST.count)
                        p.op("tensor", lambda e, i=ktb:
                             e.matmul(PSD[:, 0:T], cdft[:], i, start=True,
                                      stop=True, skip_group_check=True),
                             waits=[ktb_h[h]] + rD0.require(), inc=ST)
                        kc_w = (ST, ST.count)
                        p.op("tensor", lambda e, i=ktb:
                             e.matmul(PSD[:, T:2 * T], sdft[:], i, start=True,
                                      stop=True, skip_group_check=True),
                             waits=rD1.require(), inc=ST)
                        ks_w = (ST, ST.count)

                        dbase = M_DFT + (h % 2) * 4 * T
                        qcb = HBF[:, dbase:dbase + T]
                        qsb = HBF[:, dbase + T:dbase + 2 * T]
                        kcb = HBF[:, dbase + 2 * T:dbase + 3 * T]
                        ksb = HBF[:, dbase + 3 * T:dbase + 4 * T]
                        pg = prodsem.get(h - 2, {})
                        p.op("vector", lambda e, o=qcb:
                             e.tensor_copy(o, PSC[:, 0:T]),
                             waits=[qc_w, pg.get(3)], inc=SV)
                        rC0.set([(SV, SV.count)])
                        qcb_h[h] = (SV, SV.count)
                        p.op("vector", lambda e, o=kcb:
                             e.tensor_copy(o, PSD[:, 0:T]),
                             waits=[kc_w], inc=SV)
                        rD0.set([(SV, SV.count)])
                        kcb_h = (SV, SV.count)
                        p.op("scalar", lambda e, o=qsb:
                             e.activation(o, PSC[:, T:2 * T], AF.Copy),
                             waits=[qs_w, pg.get(2), pg.get(4)], inc=SA)
                        rC1.set([(SA, SA.count)])
                        qsb_h[h] = (SA, SA.count)
                        p.op("scalar", lambda e, o=ksb:
                             e.activation(o, PSD[:, T:2 * T], AF.Copy),
                             waits=[ks_w, pg.get(2), pg.get(3)], inc=SA)
                        rD1.set([(SA, SA.count)])
                        ksb_h[h] = (SA, SA.count)

                        pbase = M_P + (h % 2) * 4 * T
                        prods = [HBF[:, pbase + j * T:pbase + (j + 1) * T]
                                 for j in range(4)]
                        slot_free = vh_mm_h.get(h - 2)
                        ps_ = {}
                        p.op("vector", lambda e, o=prods[0], a_=qcb, b_=kcb:
                             e.tensor_mul(o, a_, b_),
                             waits=[slot_free], inc=SV)
                        ps_[1] = (SV, SV.count)
                        p.op("gpsimd", lambda e, o=prods[1], a_=qsb, b_=ksb:
                             e.tensor_mul(o, a_, b_),
                             waits=[qsb_h[h], ksb_h[h], slot_free], inc=SP)
                        ps_[2] = (SP, SP.count)
                        p.op("vector", lambda e, o=prods[2], a_=qcb, b_=ksb:
                             e.tensor_mul(o, a_, b_),
                             waits=[ksb_h[h], slot_free], inc=SV)
                        ps_[3] = (SV, SV.count)
                        p.op("vector", lambda e, o=prods[3], a_=qsb, b_=kcb:
                             e.tensor_mul(o, a_, b_),
                             waits=[qsb_h[h], slot_free], inc=SV)
                        ps_[4] = (SV, SV.count)
                        prodsem[h] = ps_
                        lhs_list = [cdft, ncdft, sdft, sdft]
                        pws = [ps_[1], ps_[2], ps_[3], ps_[4]]
                        for j in range(4):
                            p.op("tensor", lambda e, l=lhs_list[j], r=prods[j],
                                 st=(j == 0), sp=(j == 3):
                                 e.matmul(PSA[:, 0:T], l[:], r, start=st,
                                          stop=sp, skip_group_check=True),
                                 waits=[pws[j]] +
                                       (rA0.require() if j == 0 else []),
                                 inc=ST if j == 3 else None)
                        vh_w = (ST, ST.count)
                        vh_mm_h[h] = vh_w
                        p.op("tensor", lambda e, h_=h:
                             e.matmul(PSA[:, T:2 * T],
                                      maskB[:, h_ * 128:(h_ + 1) * 128],
                                      SQB[:, 0:T], start=True, stop=True,
                                      skip_group_check=True),
                             waits=[rows_w] + rA1.require(), inc=ST)
                        bcq_w = (ST, ST.count)
                        sqqb = HBF[:, M_SQ + (h % 2) * T:M_SQ + (h % 2 + 1) * T]
                        p.op("scalar", lambda e, o=sqqb:
                             e.activation(o, PSA[:, T:2 * T], AF.Copy),
                             waits=[bcq_w, vh_mul_h.get(h - 2)], inc=SA)
                        sqqb_w = (SA, SA.count)
                        rA1.set([sqqb_w])
                        p.op("vector", lambda e, h_=h, s_a=sqqb:
                             e.tensor_mul(VHBF[:, h_ * T:(h_ + 1) * T],
                                          PSA[:, 0:T], s_a),
                             waits=[vh_w, sqqb_w], inc=SV)
                        rA0.set([(SV, SV.count)])
                        vh_mul_h[h] = (SV, SV.count)
                        vh_w_head.append((SV, SV.count))
                        ck(f"Ah{h}", b, ti)
                vh_done = (SV, SV.count)
                ck("A", b, ti)

                pair_r = [Region(), Region()]
                pair_r[0].set(rA0.require() + rA1.require())
                pair_r[1].set(rC0.require() + rC1.require())
                x2a_ws = []
                for g in range(OGD):
                    sl, w_w = load_strip("wo", wo_b, g, KD * 256)
                    ps = PSA if g % 2 == 0 else PSC
                    waits = [w_w] + pair_r[g % 2].require()
                    for k in range(KD):
                        if g < 2:
                            kw = (waits + [vh_w_head[k]]) if k == 0 else \
                                [vh_w_head[k]]
                        else:
                            kw = waits if k == 0 else ()
                        for oc in range(2):
                            lhsT = WS[:, sl * SLOTW + k * 256 + oc * 128:
                                      sl * SLOTW + k * 256 + (oc + 1) * 128]
                            p.op("tensor",
                                 lambda e, o=ps[:, oc * T:(oc + 1) * T],
                                 l=lhsT, r=VHBF[:, k * T:(k + 1) * T],
                                 st=(k == 0), sp=(k == KD - 1):
                                 e.matmul(o, l, r, start=st, stop=sp,
                                          skip_group_check=True),
                                 waits=kw if oc == 0 else (),
                                 inc=ST if (k == KD - 1 and oc == 1) else None)
                    mm_w = (ST, ST.count)
                    rSLOT[sl].set([mm_w])
                    cb = 2 * g
                    p.op("vector", lambda e, cb_=cb, ps_=ps:
                         e.tensor_add(A[:, cb_ * T:(cb_ + 2) * T], ps_[:, 0:1024],
                                      B[:, cb_ * T:(cb_ + 2) * T]),
                         waits=[mm_w, xf_w], inc=SV)
                    x2a_w = (SV, SV.count)
                    pair_r[g % 2].set([x2a_w])
                    p.op("scalar", lambda e, cb_=cb:
                         e.activation(HBF[:, cb_ * T:(cb_ + 2) * T],
                                      A[:, cb_ * T:(cb_ + 2) * T], AF.Copy),
                         waits=[x2a_w] + ([vh_done] if g == 0 else []), inc=SA)
                    p.op("scalar", lambda e, cb_=cb:
                         e.activation(HBF[:, SH + cb_ * T:SH + (cb_ + 2) * T],
                                      A[:, cb_ * T:(cb_ + 2) * T], AF.Square),
                         inc=SA)
                    w2 = (SA, SA.count)
                    x2a_ws.extend([w2, w2])

                rowb0.set(rB0.require())
                rowb1.set([rec_w])
                x1_ws, ln1_v = ln_rows_apply(
                    A, B, XBF, HBF, HBF, SH, PSD, rowb0, rowb1, rD0, rD1,
                    x2a_ws, None)
                rB_F32.set([])
                ck("B", b, ti)

                hprev = {}
                h_ws = []
                gpair = [Region(), Region()]
                gpair[0].set(pair_r[0].require())
                gpair[1].set(pair_r[1].require())
                upair = [Region(), Region()]
                upair[0].set(rowb0.require() + rowb1.require())
                upair[1].set([ln1_v])
                for g in range(OGI):
                    slg, wg_w = load_strip("wg", wg_b, g, KD * 256)
                    psg = PSA if g % 2 == 0 else PSC
                    waits = [wg_w] + gpair[g % 2].require()
                    for k in range(KD):
                        if g < 2:
                            kw = (waits + [x1_ws[k]]) if k == 0 else [x1_ws[k]]
                        else:
                            kw = waits if k == 0 else ()
                        for oc in range(2):
                            lhsT = WS[:, slg * SLOTW + k * 256 + oc * 128:
                                      slg * SLOTW + k * 256 + (oc + 1) * 128]
                            p.op("tensor",
                                 lambda e, o=psg[:, oc * T:(oc + 1) * T],
                                 l=lhsT, r=XBF[:, k * T:(k + 1) * T],
                                 st=(k == 0), sp=(k == KD - 1):
                                 e.matmul(o, l, r, start=st, stop=sp,
                                          skip_group_check=True),
                                 waits=kw if oc == 0 else (),
                                 inc=ST if (k == KD - 1 and oc == 1) else None)
                    g_w = (ST, ST.count)
                    rSLOT[slg].set([g_w])
                    slu, wu_w = load_strip("wu", wu_b, g, KD * 256)
                    psu = PSB if g % 2 == 0 else PSD
                    waitsu = [wu_w] + upair[g % 2].require()
                    for k in range(KD):
                        if g < 2:
                            kw = (waitsu + [x1_ws[k]]) if k == 0 else [x1_ws[k]]
                        else:
                            kw = waitsu if k == 0 else ()
                        for oc in range(2):
                            lhsT = WS[:, slu * SLOTW + k * 256 + oc * 128:
                                      slu * SLOTW + k * 256 + (oc + 1) * 128]
                            p.op("tensor",
                                 lambda e, o=psu[:, oc * T:(oc + 1) * T],
                                 l=lhsT, r=XBF[:, k * T:(k + 1) * T],
                                 st=(k == 0), sp=(k == KD - 1):
                                 e.matmul(o, l, r, start=st, stop=sp,
                                          skip_group_check=True),
                                 waits=kw if oc == 0 else (),
                                 inc=ST if (k == KD - 1 and oc == 1) else None)
                    u_w = (ST, ST.count)
                    rSLOT[slu].set([u_w])
                    for oc in range(2):
                        c = 2 * g + oc
                        rot = c % 2
                        sg_t = QBF[:, rot * T:(rot + 1) * T]
                        u_t = QBF[:, (2 + rot) * T:(3 + rot) * T]
                        hw = [hprev[rot]] if c >= 2 else []
                        p.op("scalar", lambda e, o=sg_t, oc_=oc, c_=c, ps_=psg:
                             e.activation(o, ps_[:, oc_ * T:(oc_ + 1) * T],
                                          AF.Silu, scale=gs[:, c_:c_ + 1]),
                             waits=[g_w] + hw, inc=SA)
                        p.op("scalar", lambda e, o=u_t, oc_=oc, c_=c, ps_=psu:
                             e.mul(o, ps_[:, oc_ * T:(oc_ + 1) * T],
                                   us[:, c_:c_ + 1]),
                             waits=[u_w], inc=SA)
                        su_w = (SA, SA.count)
                        p.op("vector", lambda e, c_=c, a_=sg_t, b_=u_t:
                             e.tensor_mul(HBF[:, c_ * T:(c_ + 1) * T], a_, b_),
                             waits=[su_w], inc=SV)
                        hprev[rot] = (SV, SV.count)
                        h_ws.append((SV, SV.count))
                    gpair[g % 2].set([(SA, SA.count)])
                    upair[g % 2].set([(SA, SA.count)])
                rXBF.set([(ST, ST.count)])
                ck("C", b, ti)

                bank_regs = [Region(), Region(), Region(), Region()]
                bank_regs[0].set(gpair[0].require())
                bank_regs[1].set(gpair[0].require())
                bank_regs[2].set(gpair[1].require())
                bank_regs[3].set(gpair[1].require())
                bank_aps = [(PSA, 0), (PSA, 1), (PSC, 0), (PSC, 1)]
                x2b_ws = []
                rQ = [Region(), Region(), Region(), Region()]
                rQ[0].set(rSLOT[0].require())
                rQ[1].set(rSLOT[0].require())
                rQ[2].set(rSLOT[1].require())
                rQ[3].set(rSLOT[1].require())
                qbase = [0, 2048, SLOTW, SLOTW + 2048]
                qctr = [0]
                qmap = {}

                def load_q(g_, qd):
                    ss = qctr[0] % 4
                    qctr[0] += 1
                    dst = WS[:, qbase[ss]:qbase[ss] + KH * 128]
                    waits = [cast_at[("wd", g_)]] + rQ[ss].require()
                    p.op("sync", lambda e, s=wd_b.ap()[g_, qd], d=dst:
                         e.dma_start(out=d, in_=s),
                         waits=waits, inc=SW, inc_n=16)
                    qmap[(g_, qd)] = (ss, (SW, SW.count))

                for qd in range(4):
                    load_q(0, qd)
                for g in range(OGDN):
                    ps, half = bank_aps[g % 4]
                    waits = [qmap[(g, 0)][1]] + bank_regs[g % 4].require()
                    for k in range(KI):
                        qd = k // KH
                        lastq = (k % KH == KH - 1)
                        ss, q_w = qmap[(g, qd)]
                        lhsT = WS[:, qbase[ss] + (k % KH) * 128:
                                  qbase[ss] + (k % KH + 1) * 128]
                        kw = list(waits) if k == 0 else \
                            ([q_w] if k % KH == 0 else [])
                        if g < 4:
                            kw = kw + [h_ws[k]]
                        p.op("tensor",
                             lambda e, o=ps[:, half * T:(half + 1) * T],
                             l=lhsT, k_=k, st=(k == 0), sp=(k == KI - 1):
                             e.matmul(o, l, HBF[:, k_ * T:(k_ + 1) * T],
                                      start=st, stop=sp, skip_group_check=True),
                             waits=kw, inc=ST if lastq else None)
                        if lastq:
                            rQ[ss].set([(ST, ST.count)])
                            nxt = g * 4 + qd + 4
                            if nxt < OGDN * 4:
                                load_q(nxt // 4, nxt % 4)
                    mm_w = (ST, ST.count)
                    c = g
                    p.op("vector", lambda e, c_=c, o=A[:, c * T:(c + 1) * T],
                         i=ps[:, half * T:(half + 1) * T]:
                         e.scalar_tensor_tensor(o, i, ds[:, c_:c_ + 1],
                                                B[:, c_ * T:(c_ + 1) * T],
                                                ALU.mult, ALU.add),
                         waits=[mm_w], inc=SV)
                    x2b_w = (SV, SV.count)
                    bank_regs[g % 4].set([x2b_w])
                    p.op("scalar", lambda e, c_=c:
                         e.activation(VHBF[:, c_ * T:(c_ + 1) * T],
                                      A[:, c_ * T:(c_ + 1) * T], AF.Copy),
                         waits=[x2b_w], inc=SA)
                    p.op("scalar", lambda e, c_=c:
                         e.activation(QBF[:, c_ * T:(c_ + 1) * T],
                                      A[:, c_ * T:(c_ + 1) * T], AF.Square),
                         waits=[hprev[0], hprev[1]] if g == 0 else (), inc=SA)
                    w2 = (SA, SA.count)
                    x2b_ws.append(w2)

                rowb0.set(upair[0].require())
                rowb1.set(upair[0].require())
                rD0.set(upair[1].require())
                rD1.set(upair[1].require())
                _, ln2_v = ln_rows_apply(
                    A, B, None, VHBF, QBF, 0, PSD, rowb0, rowb1, rD0, rD1,
                    x2b_ws, (b, s0))
                store_done = (SS, SS.count)
                rB_F32.set([store_done])
                rSLOT[0].set([mm_w])
                rSLOT[1].set([mm_w])
                prev_psa = bank_regs[0].require() + bank_regs[1].require()
                prev_psc = bank_regs[2].require() + bank_regs[3].require()
                ck("D", b, ti)
                p.barrier(only=("s_t", "s_v", "s_a", "s_p"))
    except _StopEmit:
        pass
    if _dbg:
        emit_dbg_taps()

    def fin(eng):
        eng.wait_ge(SS.h, SS.count)
    p.streams["scalar"].append(fin)
    p.finalize()
    return nc


def _prep_host(inputs):
    x = np.ascontiguousarray(np.asarray(inputs["L0_attn_x"], dtype=np.float32))
    wq = np.asarray(inputs["L0_attn_W_q"], dtype=np.float32)
    wkv = np.asarray(inputs["L0_attn_W_kv"], dtype=np.float32)
    wo = np.asarray(inputs["L0_attn_W_o"], dtype=np.float32)
    gw = np.asarray(inputs["L0_ffn_gate_w"], dtype=np.float32)
    uw = np.asarray(inputs["L0_ffn_up_w"], dtype=np.float32)
    dw = np.asarray(inputs["L0_ffn_down_w"], dtype=np.float32)
    gsv = np.asarray(inputs["L0_ffn_gate_s"], dtype=np.float32)
    usv = np.asarray(inputs["L0_ffn_up_s"], dtype=np.float32)
    dsv = np.asarray(inputs["L0_ffn_down_s"], dtype=np.float32)

    def strip_d(w, nog):
        return np.ascontiguousarray(
            w.T.reshape(KD, 128, nog, 256).transpose(2, 1, 0, 3).reshape(
                nog, 128, KD * 256))

    wd_s = np.ascontiguousarray(
        dw.T.reshape(4, KH, 128, OGDN, 128).transpose(3, 0, 2, 1, 4).reshape(
            OGDN, 4, 128, KH * 128))

    m = np.arange(128)
    ang = 2.0 * np.pi * np.outer(m, m) / 128.0
    cdft = np.cos(ang).astype(np.float32).astype(ml_dtypes.bfloat16)
    sdft = np.sin(ang).astype(np.float32).astype(ml_dtypes.bfloat16)
    ncdft = (-np.cos(ang)).astype(np.float32).astype(ml_dtypes.bfloat16)
    maskA = np.zeros((128, 16, 16), dtype=ml_dtypes.bfloat16)
    for h in range(16):
        maskA[:, h, h] = 1
    maskB = np.zeros((16, 16, 128), dtype=ml_dtypes.bfloat16)
    for h in range(16):
        maskB[h, h, :] = 1

    shared = dict(
        wq_f=strip_d(wq, OGD), wkv_f=strip_d(wkv, OGD), wo_f=strip_d(wo, OGD),
        wg_f=strip_d(gw, OGI), wu_f=strip_d(uw, OGI), wd_f=wd_s,
        gs_in=np.ascontiguousarray(gsv.reshape(KI, 128).T),
        us_in=np.ascontiguousarray(usv.reshape(KI, 128).T),
        ds_in=np.ascontiguousarray(dsv.reshape(KD, 128).T),
        cdft_in=cdft, sdft_in=sdft, ncdft_in=ncdft,
        maskA_in=np.ascontiguousarray(maskA.reshape(128, 256)),
        maskB_in=np.ascontiguousarray(maskB.reshape(16, 2048)),
    )
    in_maps = []
    for i in range(NCORES):
        xc = np.ascontiguousarray(x[BL * i:BL * (i + 1)].transpose(0, 2, 1))
        im = dict(shared)
        im["x_fm"] = xc
        in_maps.append(im)
    return in_maps


def kernel(**inputs):
    in_maps = _prep_host(inputs)
    nc = build_nc()
    res = run_bass_kernel_spmd(nc, in_maps, core_ids=list(range(NCORES)))
    out = np.empty((16, S, D), dtype=np.float32)
    for i in range(NCORES):
        o = np.asarray(res.results[i]["out_fm"])
        out[BL * i:BL * (i + 1)] = o.transpose(0, 2, 1)
    return out
